# version 03
# baseline (speedup 1.0000x reference)
import math
import numpy as np
import ml_dtypes
import concourse.bass as bass
import concourse.mybir as mybir
from concourse.bass_utils import run_bass_kernel_spmd
from concourse.alu_op_type import AluOpType

F32 = mybir.dt.float32
BF16 = mybir.dt.bfloat16
I32 = mybir.dt.int32
I32 = mybir.dt.int32
AF = mybir.ActivationFunctionType
BF = ml_dtypes.bfloat16
AX = mybir.AxisListType.X

D = 2048
NT = 4096
S = 8192
NE = D // 128
TB = 512
NB = NT // TB
SQD = 1.0 / math.sqrt(128.0)
LNEPS = 1e-5

def build_l0(nc, debug=False, stop_after=None, fx=None):
    fused = fx is not None
    x0 = nc.dram_tensor("x0", [D, NT], F32, kind="ExternalInput").ap()
    wq = nc.dram_tensor("wq", [NE, D, 128], F32, kind="ExternalInput").ap()
    wk = nc.dram_tensor("wk", [NE, D, 128], F32, kind="ExternalInput").ap()
    wv = nc.dram_tensor("wv", [NE, D, 128], F32, kind="ExternalInput").ap()
    wo = nc.dram_tensor("wo", [NE, 128, D], F32, kind="ExternalInput").ap()
    wf = nc.dram_tensor("wf", [NE, 128, D], F32, kind="ExternalInput").ap()
    fscale = nc.dram_tensor("fscale", [128, NE], F32, kind="ExternalInput").ap()
    fbias = nc.dram_tensor("fbias", [128, NE], F32, kind="ExternalInput").ap()
    ident = nc.dram_tensor("ident", [128, 128], BF16, kind="ExternalInput").ap()
    maskadd = nc.dram_tensor("maskadd", [128, 128], F32, kind="ExternalInput").ap()
    x1o = nc.dram_tensor("x1", [D, NT], F32,
                         kind="Internal" if fused else "ExternalOutput").ap()
    if fused:
        fx.aps["x1"] = x1o

    qd = nc.dram_tensor("qd", [D, NT], BF16, kind="Internal").ap()
    kd = nc.dram_tensor("kd", [D, NT], BF16, kind="Internal").ap()
    vd = nc.dram_tensor("vd", [D, NT], BF16, kind="Internal").ap()
    od = nc.dram_tensor("od", [D, NT], BF16, kind="Internal").ap()
    x1a_f = nc.dram_tensor("x1a_f", [D, NT], F32, kind="Internal").ap()
    x1a_b = nc.dram_tensor("x1a_b", [D, NT], BF16, kind="Internal").ap()

    if debug:
        dbg_q = nc.dram_tensor("dbg_q", [D, NT], BF16, kind="ExternalOutput").ap()
        dbg_od = nc.dram_tensor("dbg_od", [D, NT], BF16, kind="ExternalOutput").ap()
        dbg_x1a = nc.dram_tensor("dbg_x1a", [D, NT], F32, kind="ExternalOutput").ap()

    def fmview(dram_ap, t0, t1):
        return dram_ap[:, t0:t1].rearrange("(c p) t -> p c t", p=128)

    def sbv(sbt):
        return sbt[:, :].rearrange("p (c t) -> p c t", c=NE)

    import contextlib
    es = contextlib.ExitStack()
    with es:
        pfx = fx.prefix if fused else ""
        sb = lambda name, shape, dt: es.enter_context(nc.sbuf_tensor(pfx + name, shape, dt))
        psd = lambda name, shape, dt: es.enter_context(nc.psum_tensor(pfx + name, shape, dt))
        if fused:
            semh = lambda name: fx.es.enter_context(nc.semaphore(pfx + name))
            block = fx.block
        else:
            semh = lambda name: es.enter_context(nc.semaphore(name))
            block = es.enter_context(nc.Block())

        pA = [psd("pA0", [128, 512], F32), psd("pA1", [128, 512], F32)]
        pG = [psd("pG0", [128, 512], F32), psd("pG1", [128, 512], F32)]
        pH = [psd("pH0", [128, 1024], BF16), psd("pH1", [128, 1024], BF16)]
        pQ = [psd("pQ0", [128, 512], F32), psd("pQ1", [128, 512], F32)]

        IDT = sb("IDT", [128, 128], BF16)
        MSK = sb("MSK", [128, 128], F32)
        ONE1F = sb("ONE1F", [1, 128], F32)
        ONEK = sb("ONEK", [128, 1], F32)
        FSC = sb("FSC", [128, NE], F32)
        FBI = sb("FBI", [128, NE], F32)
        EPS1 = sb("EPS1", [1, 1], F32)

        sLD = semh("sLD")
        sX0 = semh("sX0")
        sWTa = semh("sWTa"); sWTb = semh("sWTb")
        sMM = semh("sMM"); sEV = semh("sEV"); sSC = semh("sSC")
        sSTa = semh("sSTa")
        sSTb = semh("sSTb")
        sFR = semh("sFR")
        sG1 = semh("sG1"); sG2 = semh("sG2"); sG3 = semh("sG3"); sG4 = semh("sG4")
        sG5 = semh("sG5"); sG6 = semh("sG6"); sG7 = semh("sG7"); sG8 = semh("sG8")
        sG9 = semh("sG9"); sGA = semh("sGA"); sGB = semh("sGB")
        sSQ = semh("sSQ")
        sWT = [sWTa, sWTb]

        def a0_sync(E):
            E.dma_start(out=IDT[:, :], in_=ident).then_inc(sLD, 16)
            E.dma_start(out=MSK[:, :], in_=maskadd).then_inc(sLD, 16)
            E.dma_start(out=FSC[:, :], in_=fscale).then_inc(sLD, 16)
            E.dma_start(out=FBI[:, :], in_=fbias).then_inc(sLD, 16)
        block.sync(a0_sync)
        LD0 = 64

        with (
            nc.sbuf_tensor(pfx + "XB0", [128, NE * NT], BF16) as XB0,
            nc.sbuf_tensor(pfx + "WS0", [128, NE * 128], BF16) as WS0,
            nc.sbuf_tensor(pfx + "WS1", [128, NE * 128], BF16) as WS1,
            nc.sbuf_tensor(pfx + "OSTa", [128, NT], BF16) as OSTa,
            nc.sbuf_tensor(pfx + "OSTb", [128, NT], BF16) as OSTb,
        ):
            WS = [WS0, WS1]
            OST = [OSTa, OSTb]

            def a1_gpsimd(E):
                E.memset(EPS1[:, :], LNEPS)
                E.memset(ONE1F[:, :], 1.0)
                E.memset(ONEK[:, :], 1.0)
                E.dma_start(out=XB0[:, :].rearrange("p (c t) -> p c t", c=NE),
                            in_=x0.rearrange("(c p) t -> p c t", p=128)).then_inc(sX0, 16)
                for idx in range(3 * NE + 2):
                    if idx >= 2:
                        sidx = idx - 2
                        dst = (qd, kd, vd)[sidx // NE]
                        e = sidx % NE
                        E.wait_ge(sEV, 8 * (sidx + 1))
                        E.dma_start(out=dst[e * 128:(e + 1) * 128, :],
                                    in_=OST[sidx % 2][:, :]).then_inc(sSTa, 16)
                    if idx < 3 * NE:
                        wap = (wq, wk, wv)[idx // NE]
                        e = idx % NE
                        if idx >= 2:
                            E.wait_ge(sEV, 8 * (idx - 1))
                        E.dma_start(out=WS[idx % 2][:, :].rearrange("p (c e) -> p c e", c=NE),
                                    in_=wap[e].rearrange("(c p) e -> p c e", p=128)
                                    ).then_inc(sWT[idx % 2], 16)
            block.gpsimd(a1_gpsimd)

            def a1_tensor(E):
                E.wait_ge(sX0, 16)
                for idx in range(3 * NE):
                    E.wait_ge(sWT[idx % 2], 16 * (idx // 2 + 1))
                    buf = WS[idx % 2]
                    for t in range(NB):
                        g = idx * NB + t
                        if g >= 2:
                            E.wait_ge(sEV, g - 1)
                        last = None
                        for kc in range(NE):
                            last = E.matmul(pA[g % 2][:, :], buf[:, kc * 128:(kc + 1) * 128],
                                            XB0[:, kc * NT + t * TB: kc * NT + (t + 1) * TB],
                                            start=(kc == 0), stop=(kc == NE - 1))
                        last.then_inc(sMM, 1)
            block.tensor(a1_tensor)

            def a1_vector(E):
                for idx in range(3 * NE):
                    for t in range(NB):
                        g = idx * NB + t
                        E.wait_ge(sMM, g + 1)
                        if t == 0 and idx >= 2:
                            E.wait_ge(sSTa, 16 * idx)
                        E.tensor_copy(OST[idx % 2][:, t * TB:(t + 1) * TB],
                                      pA[g % 2][:, :]).then_inc(sEV, 1)
            block.vector(a1_vector)

        MM_A1 = 3 * NE * NB
        EV_A1 = 3 * NE * NB
        FR_A1 = 3 * NE
        STA_A1 = 16 * 3 * NE
        STA_A1_EARLY = STA_A1
        WTA_A1 = 16 * 24
        WTB_A1 = 16 * 24

        if stop_after == 'a1':
            def fin_a1(E):
                E.wait_ge(sSTa, STA_A1_EARLY)
            block.sync(fin_a1)
            return nc
        with (
            nc.sbuf_tensor(pfx + "QS", [128, NE * TB], BF16) as QS,
            nc.sbuf_tensor(pfx + "KS", [128, NE * TB], BF16) as KS,
            nc.sbuf_tensor(pfx + "VS", [128, NE * TB], BF16) as VS,
            nc.sbuf_tensor(pfx + "OFM", [128, NE * TB], BF16) as OFM,
            nc.sbuf_tensor(pfx + "QG", [128, 2 * 128], BF16) as QG,
            nc.sbuf_tensor(pfx + "KG", [128, 2 * 128], BF16) as KG,
            nc.sbuf_tensor(pfx + "VG", [128, 2 * 128], BF16) as VG,
            nc.sbuf_tensor(pfx + "SCM", [128, 2 * 128], F32) as SCM,
            nc.sbuf_tensor(pfx + "MX", [128, 2], F32) as MX,
            nc.sbuf_tensor(pfx + "NMX", [128, 2], F32) as NMX,
            nc.sbuf_tensor(pfx + "SM", [128, 2], F32) as SM,
            nc.sbuf_tensor(pfx + "RSM", [128, 2], F32) as RSM,
            nc.sbuf_tensor(pfx + "PRB", [128, 2 * 128], BF16) as PRB,
            nc.sbuf_tensor(pfx + "ATT", [128, 2 * 128], BF16) as ATT,
            nc.sbuf_tensor(pfx + "VGT", [128, 2 * 128], BF16) as VGT,
            nc.sbuf_tensor(pfx + "OGB", [128, 2 * 128], BF16) as OGB,
        ):
            NG = TB // 8

            def grp(src, gi):
                return (src[:, :].rearrange("p (c t) -> p c t", c=NE)
                        [:, :, gi * 8:(gi + 1) * 8].rearrange("p c t -> p t c"))

            def g128(t2d, c0):
                return t2d[:, c0:c0 + 128].rearrange("p (t c) -> p t c", t=8)

            def a2a_sync(E):
                for t in range(NB):
                    if t > 0:
                        E.wait_ge(sG1, NG * t)
                    else:
                        E.wait_ge(sSTa, STA_A1)
                    E.dma_start(out=sbv(QS), in_=fmview(qd, t * TB, (t + 1) * TB)).then_inc(sLD, 16)
                    E.dma_start(out=sbv(KS), in_=fmview(kd, t * TB, (t + 1) * TB)).then_inc(sLD, 16)
                    E.dma_start(out=sbv(VS), in_=fmview(vd, t * TB, (t + 1) * TB)).then_inc(sLD, 16)
            block.sync(a2a_sync)

            def a2a_gpsimd(E):
                for t in range(NB):
                    E.wait_ge(sLD, LD0 + 48 * (t + 1))
                    for gi in range(NG):
                        gg = t * NG + gi
                        sl = gg % 2
                        if gg >= 2:
                            E.wait_ge(sG6, gg - 1)
                        E.tensor_copy(g128(QG, sl * 128), grp(QS, gi))
                        E.tensor_copy(g128(KG, sl * 128), grp(KS, gi))
                        E.tensor_copy(g128(VG, sl * 128), grp(VS, gi)).then_inc(sG1, 1)
                    E.wait_ge(sGB, NG * (t + 1))
                    E.dma_start(out=fmview(od, t * TB, (t + 1) * TB), in_=sbv(OFM)).then_inc(sSTa, 16)
            block.gpsimd(a2a_gpsimd)

            def a2a_tensor(E):
                for gg in range(NB * NG):
                    sl = gg % 2
                    c0, c1 = sl * 128, (sl + 1) * 128
                    E.wait_ge(sG1, gg + 1)
                    if gg >= 2:
                        E.wait_ge(sG3, gg - 1)
                    E.matmul(pG[sl][:, 0:128], QG[:, c0:c1], KG[:, c0:c1],
                             start=True, stop=True).then_inc(sG2, 1)
                    E.wait_ge(sG4, gg + 1)
                    if gg >= 2:
                        E.wait_ge(sG7, gg - 1)
                        E.wait_ge(sGB, gg - 1)
                    E.transpose(pH[sl][:, 0:128], PRB[:, c0:c1], IDT[:, :])
                    E.transpose(pH[sl][:, 128:256], VG[:, c0:c1], IDT[:, :]).then_inc(sG6, 1)
                    E.wait_ge(sG7, gg + 1)
                    if gg >= 2:
                        E.wait_ge(sG9, gg - 1)
                    E.matmul(pQ[sl][:, 0:128], ATT[:, c0:c1], VGT[:, c0:c1],
                             start=True, stop=True).then_inc(sG8, 1)
                    E.wait_ge(sG9, gg + 1)
                    E.transpose(pH[sl][:, 256:384], OGB[:, c0:c1], IDT[:, :]).then_inc(sGA, 1)
            block.tensor(a2a_tensor)

            def a2a_vector(E):
                for gg in range(NB * NG):
                    t, gi = divmod(gg, NG)
                    sl = gg % 2
                    c0, c1 = sl * 128, (sl + 1) * 128
                    E.wait_ge(sG2, gg + 1)
                    if gg >= 2:
                        E.wait_ge(sG4, gg - 1)
                    E.scalar_tensor_tensor(SCM[:, c0:c1], pG[sl][:, 0:128], SQD, MSK[:, :],
                                           op0=AluOpType.mult, op1=AluOpType.add)
                    E.tensor_reduce(MX[:, sl:sl + 1], SCM[:, c0:c1], axis=AX, op=AluOpType.max)
                    E.tensor_scalar(NMX[:, sl:sl + 1], MX[:, sl:sl + 1], -1.0, None,
                                    op0=AluOpType.mult).then_inc(sG3, 1)
                    E.wait_ge(sG4, gg + 1)
                    E.reciprocal(RSM[:, sl:sl + 1], SM[:, sl:sl + 1]).then_inc(sG5, 1)
                    E.wait_ge(sG6, gg + 1)
                    E.tensor_copy(ATT[:, c0:c1], pH[sl][:, 0:128])
                    E.tensor_copy(VGT[:, c0:c1], pH[sl][:, 128:256]).then_inc(sG7, 1)
                    E.wait_ge(sG8, gg + 1)
                    E.wait_ge(sG5, gg + 1)
                    E.tensor_scalar(OGB[:, c0:c1], pQ[sl][:, 0:128], RSM[:, sl:sl + 1], None,
                                    op0=AluOpType.mult).then_inc(sG9, 1)
                    E.wait_ge(sGA, gg + 1)
                    if gi == 0 and t > 0:
                        E.wait_ge(sSTa, STA_A1 + 16 * t)
                    E.tensor_copy(grp(OFM, gi), g128(pH[sl], 256)).then_inc(sGB, 1)
            block.vector(a2a_vector)

            def a2a_scalar(E):
                for gg in range(NB * NG):
                    sl = gg % 2
                    c0, c1 = sl * 128, (sl + 1) * 128
                    E.wait_ge(sG3, gg + 1)
                    if gg >= 2:
                        E.wait_ge(sG6, gg - 1)
                    E.activation(PRB[:, c0:c1], SCM[:, c0:c1], AF.Exp,
                                 bias=NMX[:, sl:sl + 1], scale=1.0,
                                 accum_out=SM[:, sl:sl + 1]).then_inc(sG4, 1)
            block.scalar(a2a_scalar)

        LD_A2a = LD0 + 48 * NB
        STA_A2a = STA_A1 + 16 * NB
        NGRP = NB * (TB // 8)
        if stop_after == 'a2a':
            def fin_a2a(E):
                E.wait_ge(sSTa, STA_A2a)
            block.sync(fin_a2a)
            return nc

        with (
            nc.sbuf_tensor(pfx + "WR", [128, NE * D], BF16) as WR,
            nc.sbuf_tensor(pfx + "INB", [128, NE * TB], BF16) as INB,
            nc.sbuf_tensor(pfx + "XR", [128, NE * TB], F32) as XR,
            nc.sbuf_tensor(pfx + "XA", [128, NE * TB], F32) as XA,
            nc.sbuf_tensor(pfx + "XAB", [128, NE * TB], BF16) as XAB,
            nc.sbuf_tensor(pfx + "SQ", [128, 2 * TB], F32) as SQ,
            nc.sbuf_tensor(pfx + "FF", [128, TB], F32) as FF,
            nc.sbuf_tensor(pfx + "MEAN", [1, TB], F32) as MEAN,
            nc.sbuf_tensor(pfx + "M2", [1, TB], F32) as M2,
            nc.sbuf_tensor(pfx + "VARV", [1, TB], F32) as VARV,
            nc.sbuf_tensor(pfx + "LNV", [1, TB], F32) as LNV,
            nc.sbuf_tensor(pfx + "RSTD", [1, TB], F32) as RSTD,
            nc.sbuf_tensor(pfx + "NMRS", [1, TB], F32) as NMRS,
            nc.sbuf_tensor(pfx + "ABC", [128, TB], F32) as ABC,
            nc.sbuf_tensor(pfx + "BBC", [128, TB], F32) as BBC,
        ):
            NSTAGE = 1 if stop_after == 'a2b' else 2
            EVB0 = 16 + 4 + 1 + 2 + 16 + 16
            EVB1 = 16 + 4 + 1 + 2 + 16
            def evb(stage):
                return EVB0 if stage == 0 else EVB1
            MM_BLK = 16 + 1
            SC_BLK = 16 + 2

            bases = {}
            for stage in range(2):
                bases[stage] = dict(
                    mm=MM_A1 + stage * NB * MM_BLK,
                    ev=EV_A1 + stage * NB * EVB0,
                    sc=stage * NB * SC_BLK,
                    ld=LD_A2a + stage * NB * 32,
                    wta=WTA_A1 + stage * 256,
                    stb=stage * NB * 32,
                )

            def P_(stage):
                if stage == 0:
                    return dict(win=wo, src_b=od, res_f=x0, ffn=False, out_f=x1a_f, out_b=x1a_b)
                return dict(win=wf, src_b=x1a_b, res_f=x1a_f, ffn=True, out_f=x1o, out_b=None)

            def fr_of(stage, t):
                return stage * NB + t + 1

            def emit_gpsimd(E):
                for stage in range(NSTAGE):
                    B = bases[stage]; P = P_(stage)
                    if stage == 1:
                        E.wait_ge(sFR, NB)
                    for kc in range(NE):
                        E.dma_start(out=WR[:, kc * D:(kc + 1) * D],
                                    in_=P['win'][kc]).then_inc(sWTa, 16)
                    for t in range(NB):
                        E.wait_ge(sEV, B['ev'] + (t + 1) * evb(stage))
                        E.dma_start(out=fmview(P['out_f'], t * TB, (t + 1) * TB),
                                    in_=sbv(XA)).then_inc(sSTb, 16)
                        if stage == 0:
                            E.dma_start(out=fmview(P['out_b'], t * TB, (t + 1) * TB),
                                        in_=sbv(XAB)).then_inc(sSTb, 16)
                        else:
                            E.dma_start(out=fmview(x1a_b, t * TB, (t + 1) * TB),
                                        in_=sbv(XAB)).then_inc(sSTb, 16)
            block.gpsimd(emit_gpsimd)

            def emit_sync(E):
                for stage in range(NSTAGE):
                    B = bases[stage]; P = P_(stage)
                    for t in range(NB):
                        if t > 0:
                            E.wait_ge(sEV, B['ev'] + t * evb(stage))
                            E.wait_ge(sSC, B['sc'] + t * SC_BLK)
                        elif stage == 1:
                            E.wait_ge(sEV, bases[0]['ev'] + NB * EVB0)
                            E.wait_ge(sSC, bases[0]['sc'] + NB * SC_BLK)
                        if stage == 0:
                            if t == 0:
                                E.wait_ge(sGB, NGRP)
                            E.wait_ge(sSTa, STA_A1 + 16 * (t + 1))
                        else:
                            E.wait_ge(sSTb, 32 * NB)
                        E.dma_start(out=sbv(INB),
                                    in_=fmview(P['src_b'], t * TB, (t + 1) * TB)).then_inc(sLD, 16)
                        E.dma_start(out=sbv(XR),
                                    in_=fmview(P['res_f'], t * TB, (t + 1) * TB)).then_inc(sLD, 16)
            block.sync(emit_sync)

            def emit_tensor(E):
                for stage in range(NSTAGE):
                    B = bases[stage]
                    E.wait_ge(sWTa, B['wta'] + 256)
                    for t in range(NB):
                        E.wait_ge(sLD, B['ld'] + 32 * (t + 1))
                        for e in range(NE):
                            ge = t * NE + e
                            gp = stage * NB * NE + ge - 2
                            if gp >= 0:
                                sp, rem = divmod(gp, NB * NE)
                                E.wait_ge(sEV, bases[sp]['ev'] + (rem // NE) * evb(sp) + (rem % NE) + 1)
                            else:
                                E.wait_ge(sEV, EV_A1)
                            last = None
                            for kc in range(NE):
                                last = E.matmul(pA[ge % 2][:, :],
                                                WR[:, kc * D + e * 128: kc * D + (e + 1) * 128],
                                                INB[:, kc * TB:(kc + 1) * TB],
                                                start=(kc == 0), stop=(kc == NE - 1))
                            last.then_inc(sMM, 1)
                        for e in range(NE):
                            E.wait_ge(sEV, B['ev'] + t * evb(stage) + e + 1)
                            mmi = E.matmul(pG[0][0:1, :], ONEK[:, :], XR[:, e * TB:(e + 1) * TB],
                                           start=(e == 0), stop=(e == NE - 1))
                            if e == NE - 1:
                                mmi.then_inc(sMM, 1)
                        for e in range(NE):
                            E.wait_ge(sSC, B['sc'] + t * SC_BLK + e + 1)
                            E.matmul(pG[1][0:1, :], ONEK[:, :], SQ[:, (e % 2) * TB:(e % 2 + 1) * TB],
                                     start=(e == 0), stop=(e == NE - 1)).then_inc(sSQ, 1)
                        E.wait_ge(sEV, B['ev'] + t * evb(stage) + 16 + 4 + 1)
                        E.matmul(pQ[0][:, :], ONE1F[:, :], RSTD[:, :], start=True, stop=True)
                        E.matmul(pQ[1][:, :], ONE1F[:, :], NMRS[:, :],
                                 start=True, stop=True).then_inc(sFR, 1)
            block.tensor(emit_tensor)

            def emit_vector(E):
                for stage in range(NSTAGE):
                    B = bases[stage]; P = P_(stage)
                    for t in range(NB):
                        for e in range(NE):
                            E.wait_ge(sMM, B['mm'] + t * MM_BLK + e + 1)
                            ge = t * NE + e
                            if P['ffn']:
                                E.tensor_scalar(FF[:, :], pA[ge % 2][:, :],
                                                FSC[:, e:e + 1], FBI[:, e:e + 1],
                                                op0=AluOpType.mult, op1=AluOpType.add)
                                E.tensor_tensor(XR[:, e * TB:(e + 1) * TB], FF[:, :],
                                                XR[:, e * TB:(e + 1) * TB],
                                                op=AluOpType.add).then_inc(sEV, 1)
                            else:
                                E.tensor_tensor(XR[:, e * TB:(e + 1) * TB], pA[ge % 2][:, :],
                                                XR[:, e * TB:(e + 1) * TB],
                                                op=AluOpType.add).then_inc(sEV, 1)
                        E.wait_ge(sMM, B['mm'] + t * MM_BLK + 17)
                        E.wait_ge(sSQ, (stage * NB + t + 1) * NE)
                        E.tensor_scalar(MEAN[:, :], pG[0][0:1, :], 1.0 / D, None,
                                        op0=AluOpType.mult).then_inc(sEV, 1)
                        E.tensor_tensor(M2[:, :], MEAN[:, :], MEAN[:, :],
                                        op=AluOpType.mult).then_inc(sEV, 1)
                        E.tensor_scalar(VARV[:, :], pG[1][0:1, :], 1.0 / D, None,
                                        op0=AluOpType.mult).then_inc(sEV, 1)
                        E.tensor_tensor(VARV[:, :], VARV[:, :], M2[:, :],
                                        op=AluOpType.subtract).then_inc(sEV, 1)
                        E.wait_ge(sSC, B['sc'] + t * SC_BLK + 18)
                        E.scalar_tensor_tensor(NMRS[:, :], MEAN[:, :], -1.0, RSTD[:, :],
                                               op0=AluOpType.mult, op1=AluOpType.mult).then_inc(sEV, 1)
                        E.wait_ge(sFR, fr_of(stage, t))
                        E.tensor_copy(ABC[:, :], pQ[0][:, :]).then_inc(sEV, 1)
                        E.tensor_copy(BBC[:, :], pQ[1][:, :]).then_inc(sEV, 1)
                        for e in range(NE):
                            if e == 0 and (t > 0 or stage == 1):
                                E.wait_ge(sSTb, B['stb'] + 32 * t)
                            E.tensor_tensor(XA[:, e * TB:(e + 1) * TB], XR[:, e * TB:(e + 1) * TB],
                                            ABC[:, :], op=AluOpType.mult)
                            E.tensor_tensor(XA[:, e * TB:(e + 1) * TB], XA[:, e * TB:(e + 1) * TB],
                                            BBC[:, :], op=AluOpType.add).then_inc(sEV, 1)
                        if stage == 0:
                            for e in range(NE):
                                E.tensor_copy(XAB[:, e * TB:(e + 1) * TB],
                                              XA[:, e * TB:(e + 1) * TB]).then_inc(sEV, 1)
            block.vector(emit_vector)

            def emit_scalar(E):
                for stage in range(NSTAGE):
                    B = bases[stage]
                    for t in range(NB):
                        for e in range(NE):
                            E.wait_ge(sEV, B['ev'] + t * evb(stage) + e + 1)
                            if e >= 2:
                                E.wait_ge(sSQ, (stage * NB + t) * NE + e - 1)
                            E.activation(SQ[:, (e % 2) * TB:(e % 2 + 1) * TB],
                                         XR[:, e * TB:(e + 1) * TB], AF.Square).then_inc(sSC, 1)
                        E.wait_ge(sEV, B['ev'] + t * evb(stage) + 16 + 4)
                        E.activation(LNV[:, :], VARV[:, :], AF.Ln, bias=EPS1[0:1, 0:1]).then_inc(sSC, 1)
                        E.activation(RSTD[:, :], LNV[:, :], AF.Exp, scale=-0.5).then_inc(sSC, 1)
            block.scalar(emit_scalar)

            def fin_sync(E):
                E.wait_ge(sSTb, NSTAGE * NB * 32)
                if debug:
                    E.dma_start(out=dbg_q[:, :], in_=qd[:, :]).then_inc(sLD, 16)
                    E.dma_start(out=dbg_od[:, :], in_=od[:, :]).then_inc(sLD, 16)
                    E.dma_start(out=dbg_x1a[:, :], in_=x1a_f[:, :]).then_inc(sLD, 16)
                    E.wait_ge(sLD, bases[1]['ld'] + NB * 32 + 48)
            block.sync(fin_sync)
    return nc



NQE = 32
NZE = 16
NWE = NQE + NZE
SB_ = S

def build_l1p(nc, fx=None):
    fused = fx is not None
    if fused:
        x1f = fx.aps["x1f"]
    else:
        x1f = nc.dram_tensor("x1f", [D, S], F32, kind="ExternalInput").ap()
    okind = "Internal" if fused else "ExternalOutput"
    wqkv = nc.dram_tensor("wqkv", [NQE, D, 128], F32, kind="ExternalInput").ap()
    wz = nc.dram_tensor("wz", [NZE, D, 128], F32, kind="ExternalInput").ap()
    wba = nc.dram_tensor("wba", [NE, 128, 32], F32, kind="ExternalInput").ap()
    cw = nc.dram_tensor("cw", [128, 128], F32, kind="ExternalInput").ap()
    qc = nc.dram_tensor("qc", [NQE * 128, S], BF16, kind=okind).ap()
    zsf = nc.dram_tensor("zsf", [D, S], BF16, kind=okind).ap()
    ba = nc.dram_tensor("ba", [S, 32], F32, kind=okind).ap()
    qraw = nc.dram_tensor("qraw", [NQE * 128, S], BF16, kind="Internal").ap()
    if fused:
        fx.aps["qc"] = qc
        fx.aps["zsf"] = zsf
        fx.aps["ba"] = ba

    HN = S // 2
    HB = HN // TB
    GH = NWE * HB
    EVH = GH + 32

    import contextlib
    es = contextlib.ExitStack()
    with es:
        pfx = fx.prefix if fused else ""
        sb = lambda name, shape, dt: es.enter_context(nc.sbuf_tensor(pfx + name, shape, dt))
        psd = lambda name, shape, dt: es.enter_context(nc.psum_tensor(pfx + name, shape, dt))
        if fused:
            semh = lambda name: fx.es.enter_context(nc.semaphore(pfx + name))
            block = fx.block
        else:
            semh = lambda name: es.enter_context(nc.semaphore(name))
            block = es.enter_context(nc.Block())

        pA = [psd("pA0", [128, 512], F32), psd("pA1", [128, 512], F32)]
        pBA = psd("pBA", [128, 512], F32)

        CW = sb("CW", [128, 128], F32)
        import contextlib as _ctl
        proj_es = _ctl.ExitStack()
        XBH = proj_es.enter_context(nc.sbuf_tensor(pfx + "XBH", [128, NE * HN], BF16))
        WS = [proj_es.enter_context(nc.sbuf_tensor(pfx + "WS0", [128, NE * 128], BF16)),
              proj_es.enter_context(nc.sbuf_tensor(pfx + "WS1", [128, NE * 128], BF16))]
        OST = [proj_es.enter_context(nc.sbuf_tensor(pfx + "OSTa", [128, HN], BF16)),
               proj_es.enter_context(nc.sbuf_tensor(pfx + "OSTb", [128, HN], BF16))]
        ZST = proj_es.enter_context(nc.sbuf_tensor(pfx + "ZST", [128, 2 * TB], F32))
        WBA = proj_es.enter_context(nc.sbuf_tensor(pfx + "WBA", [128, NE * 32], BF16))
        BAST = proj_es.enter_context(nc.sbuf_tensor(pfx + "BAST", [128, 32 * 32], F32))

        sXD = semh("sXD"); sWB = semh("sWB")
        sWL = [semh("sWL0"), semh("sWL1")]
        sMM = semh("sMM"); sEV = semh("sEV"); sSCz = semh("sSCz")
        sSTa = semh("sSTa")
        sQL = semh("sQL"); sCV = semh("sCV"); sSCc = semh("sSCc"); sSTc = semh("sSTc")
        sCW = semh("sCW")

        def wsrc(idx):
            return wqkv[idx] if idx < NQE else wz[idx - NQE]

        def l1p_sync(E):
            E.dma_start(out=CW[:, :], in_=cw).then_inc(sCW, 16)
        block.sync(l1p_sync)

        def l1p_gpsimd(E):
            E.dma_start(out=WBA[:, :].rearrange("p (c e) -> p c e", c=NE),
                        in_=wba.rearrange("c p e -> p c e")).then_inc(sWB, 16)
            for h in range(2):
                if h == 1:
                    E.wait_ge(sEV, EVH)
                E.dma_start(out=XBH[:, :].rearrange("p (c t) -> p c t", c=NE),
                            in_=x1f[:, h * HN:(h + 1) * HN].rearrange("(c p) t -> p c t", p=128)
                            ).then_inc(sXD, 16)
                for idx in range(NWE + 2):
                    if idx >= 2:
                        sidx = idx - 2
                        dst = qraw if sidx < NQE else zsf
                        row = sidx * 128 if sidx < NQE else (sidx - NQE) * 128
                        if sidx < NQE:
                            E.wait_ge(sEV, h * EVH + 8 * (sidx + 1))
                        else:
                            E.wait_ge(sSCz, h * 128 + 8 * (sidx - NQE + 1))
                        E.dma_start(out=dst[row:row + 128, h * HN:(h + 1) * HN],
                                    in_=OST[sidx % 2][:, :]).then_inc(sSTa, 16)
                    if idx < NWE:
                        if idx >= 2:
                            E.wait_ge(sEV, h * EVH + 8 * (idx - 1))
                        E.dma_start(out=WS[idx % 2][:, :].rearrange("p (c e) -> p c e", c=NE),
                                    in_=wsrc(idx).rearrange("(c p) e -> p c e", p=128)
                                    ).then_inc(sWL[idx % 2], 16)
                E.wait_ge(sEV, h * EVH + EVH)
                E.dma_start(out=ba[h * HN:(h + 1) * HN, :].rearrange("(tc p) c -> p tc c", p=128),
                            in_=BAST[:, :].rearrange("p (tc c) -> p tc c", c=32)
                            ).then_inc(sSTa, 16)
        block.gpsimd(l1p_gpsimd)

        def l1p_tensor(E):
            E.wait_ge(sWB, 16)
            for h in range(2):
                E.wait_ge(sXD, 16 * (h + 1))
                for idx in range(NWE):
                    E.wait_ge(sWL[idx % 2], 16 * (h * (NWE // 2) + idx // 2 + 1))
                    buf = WS[idx % 2]
                    for t in range(HB):
                        gidx = idx * HB + t
                        if gidx >= 2:
                            E.wait_ge(sEV, h * EVH + gidx - 1)
                        last = None
                        for kc in range(NE):
                            last = E.matmul(pA[gidx % 2][:, :], buf[:, kc * 128:(kc + 1) * 128],
                                            XBH[:, kc * HN + t * TB: kc * HN + (t + 1) * TB],
                                            start=(kc == 0), stop=(kc == NE - 1))
                        last.then_inc(sMM, 1)
                for tc in range(32):
                    if tc >= 1 or h == 1:
                        E.wait_ge(sEV, h * EVH + GH + tc)
                    last = None
                    for kc in range(NE):
                        last = E.matmul(pBA[:, 0:32],
                                        XBH[:, kc * HN + tc * 128: kc * HN + (tc + 1) * 128],
                                        WBA[:, kc * 32:(kc + 1) * 32],
                                        start=(kc == 0), stop=(kc == NE - 1))
                    last.then_inc(sMM, 1)
        block.tensor(l1p_tensor)

        def l1p_vector(E):
            for h in range(2):
                for idx in range(NWE):
                    for t in range(HB):
                        gidx = idx * HB + t
                        E.wait_ge(sMM, h * EVH + gidx + 1)
                        if t == 0 and idx >= 2:
                            E.wait_ge(sSTa, 16 * ((NWE + 1) * h + idx - 1))
                        elif t == 0 and h == 1:
                            E.wait_ge(sSTa, 16 * (NWE + 1))
                        if idx < NQE:
                            E.tensor_copy(OST[idx % 2][:, t * TB:(t + 1) * TB],
                                          pA[gidx % 2][:, :]).then_inc(sEV, 1)
                        else:
                            zidx = (idx - NQE) * HB + t
                            if zidx >= 2:
                                E.wait_ge(sSCz, h * 128 + zidx - 1)
                            E.tensor_copy(ZST[:, (zidx % 2) * TB:(zidx % 2 + 1) * TB],
                                          pA[gidx % 2][:, :]).then_inc(sEV, 1)
                for tc in range(32):
                    E.wait_ge(sMM, h * EVH + GH + tc + 1)
                    E.tensor_copy(BAST[:, tc * 32:(tc + 1) * 32], pBA[:, 0:32]).then_inc(sEV, 1)
        block.vector(l1p_vector)

        def l1p_scalar(E):
            for h in range(2):
                for zi in range(NZE):
                    idx = NQE + zi
                    for t in range(HB):
                        zidx = zi * HB + t
                        E.wait_ge(sEV, h * EVH + idx * HB + t + 1)
                        if t == 0 and idx >= 2:
                            E.wait_ge(sSTa, 16 * ((NWE + 1) * h + idx - 1))
                        E.activation(OST[idx % 2][:, t * TB:(t + 1) * TB],
                                     ZST[:, (zidx % 2) * TB:(zidx % 2 + 1) * TB],
                                     AF.Silu).then_inc(sSCz, 1)
        block.scalar(l1p_scalar)

        proj_es.close()

        QCI0 = sb("QCI0", [128, 3 + S], BF16)
        QCI1 = sb("QCI1", [128, 3 + S], BF16)
        ACC = sb("ACC", [128, S], F32)
        QCO = [sb("QCO0", [128, S], BF16), sb("QCO1", [128, S], BF16)]

        def conv_gpsimd(E):
            E.wait_ge(sEV, 2 * EVH)
            E.memset(QCI0[:, 0:3], 0.0)
            E.memset(QCI1[:, 0:3], 0.0)
            for i in range(NQE + 2):
                if i >= 2:
                    si = i - 2
                    E.wait_ge(sSCc, si + 1)
                    E.dma_start(out=qc[si * 128:(si + 1) * 128, :],
                                in_=QCO[si % 2][:, :]).then_inc(sSTc, 16)
                if i < NQE:
                    if i >= 2:
                        E.wait_ge(sCV, i - 1)
                    E.dma_start(out=(QCI0 if i % 2 == 0 else QCI1)[:, 3:3 + S],
                                in_=qraw[i * 128:(i + 1) * 128, :]).then_inc(sQL, 16)
        block.gpsimd(conv_gpsimd)

        def conv_vector(E):
            for i in range(NQE):
                QCI = QCI0 if i % 2 == 0 else QCI1
                E.wait_ge(sQL, 16 * (i + 1))
                E.wait_ge(sCW, 16)
                if i >= 1:
                    E.wait_ge(sSCc, i)
                E.tensor_scalar(ACC[:, :], QCI[:, 0:S], CW[:, 4 * i:4 * i + 1], None,
                                op0=AluOpType.mult)
                last = None
                for k in range(1, 4):
                    last = E.scalar_tensor_tensor(ACC[:, :], QCI[:, k:k + S],
                                                  CW[:, 4 * i + k:4 * i + k + 1],
                                                  ACC[:, :], op0=AluOpType.mult, op1=AluOpType.add)
                last.then_inc(sCV, 1)
        block.vector(conv_vector)

        def conv_scalar(E):
            for i in range(NQE):
                E.wait_ge(sCV, i + 1)
                if i >= 2:
                    E.wait_ge(sSTc, 16 * (i - 1))
                E.activation(QCO[i % 2][:, :], ACC[:, :], AF.Silu).then_inc(sSCc, 1)
        block.scalar(conv_scalar)

        def l1p_fin(E):
            E.wait_ge(sSTa, 2 * (NWE + 1) * 16)
            E.wait_ge(sSTc, NQE * 16)
        block.sync(l1p_fin)
    return nc



CH = 128
NCH = S // CH
NVH = 16
NKH = 8


class OpList:
    ENG = ("T", "V", "C", "G", "Y")
    SEMS = ("T", "V", "C", "G", "Y", "H")
    STREAM_OF = {"T": "T", "V": "V", "C": "C", "G": "G", "Y": "Y", "H": "G"}

    def __init__(self):
        self.ops = {e: [] for e in self.ENG}
        self.cnt = {s: 0 for s in self.SEMS}
        self.evt = {}

    def add(self, eng, fn, waits=(), name=None, inc=1, sem=None):
        sem = sem or eng
        self.cnt[sem] += inc
        if name is not None:
            self.evt[name] = (sem, self.cnt[sem])
        self.ops[eng].append((fn, tuple(w for w in waits if w is not None), inc, sem))

    def emit(self, eng, E, sems):
        self.emit_range(eng, E, sems, 0, len(self.ops[eng]))

    def emit_range(self, eng, E, sems, lo, hi):
        hwm = {}
        for fn, waits, inc, sem in self.ops[eng][lo:hi]:
            for w in waits:
                wsem, wval = self.evt[w]
                if hwm.get(wsem, -1) >= wval:
                    continue
                hwm[wsem] = wval
                E.wait_ge(sems[wsem], wval)
            inst = fn(E)
            if isinstance(inst, list):
                per = inc // len(inst)
                for it in inst:
                    it.then_inc(sems[sem], per)
            elif inst is not None:
                inst.then_inc(sems[sem], inc)


def build_l1s(nc, nch=NCH, debug=False, fx=None):
    fused = fx is not None
    if fused:
        qc = fx.aps["qc"]
        zsf = fx.aps["zsf"]
        ba = fx.aps["ba"]
    else:
        qc = nc.dram_tensor("qc", [NQE * 128, S], BF16, kind="ExternalInput").ap()
        zsf = nc.dram_tensor("zsf", [D, S], BF16, kind="ExternalInput").ap()
        ba = nc.dram_tensor("ba", [S, 32], F32, kind="ExternalInput").ap()
    adt = nc.dram_tensor("adt", [128, 32], F32, kind="ExternalInput").ap()
    nw = nc.dram_tensor("nw", [128, 128], F32, kind="ExternalInput").ap()
    msl = nc.dram_tensor("msl", [128, 128], F32, kind="ExternalInput").ap()
    msu = nc.dram_tensor("msu", [128, 128], F32, kind="ExternalInput").ap()
    muo = nc.dram_tensor("muo", [128, 128], F32, kind="ExternalInput").ap()
    identb = nc.dram_tensor("identb", [128, 128], BF16, kind="ExternalInput").ap()
    identf = nc.dram_tensor("identf", [128, 128], F32, kind="ExternalInput").ap()
    gated = nc.dram_tensor("gated", [S, D], BF16,
                           kind="Internal" if fused else "ExternalOutput").ap()
    rb16 = nc.dram_tensor("rb16", [16, 128], F32, kind="Internal").ap()
    rb8 = nc.dram_tensor("rb8", [8, 128], F32, kind="Internal").ap()
    if fused:
        fx.aps["gated"] = gated

    import contextlib
    es = contextlib.ExitStack()
    with es:
        pfx = fx.prefix if fused else ""
        sb = lambda name, shape, dt: es.enter_context(nc.sbuf_tensor(pfx + name, shape, dt))
        psd = lambda name, shape, dt: es.enter_context(nc.psum_tensor(pfx + name, shape, dt))
        if fused:
            semh = lambda name: fx.es.enter_context(nc.semaphore(pfx + name))
            block = fx.block
        else:
            semh = lambda name: es.enter_context(nc.semaphore(name))
            block = es.enter_context(nc.Block())

        FBK = [psd(f"pF{i}", [128, 512], F32) for i in range(6)]
        BBK = [psd(f"pB{i}", [128, 1024], BF16) for i in range(2)]

        IDT = sb("IDT", [128, 128], BF16)
        IDTF = sb("IDTF", [128, 128], F32)
        MSL = sb("MSL", [128, 128], F32)
        MSU = sb("MSU", [128, 128], F32)
        MUO = sb("MUO", [128, 128], F32)
        NWB = sb("NWB", [128, 128], F32)
        ADT = sb("ADT", [128, 32], F32)
        EXA = sb("EXA", [128, 16], F32)
        C30 = sb("C30", [128, 128], F32)
        ONE1F = sb("ONE1F", [1, 128], F32)
        ONEK = sb("ONEK", [128, 1], F32)
        ONEC = sb("ONEC", [128, 1], F32)
        EPS6 = sb("EPS6", [1, 1], F32)
        EPS6C = sb("EPS6C", [128, 1], F32)
        LNK = sb("LNK", [1, 1], F32)
        TRASHB = sb("TRASHB", [128, 128], BF16)
        SPS = sb("SPS", [128, 64], F32)

        QKV = [sb(f"QKV{i}", [128, NQE * CH], BF16) for i in range(2)]
        ZSC = [sb(f"ZSC{i}", [128, D], BF16) for i in range(2)]
        BAT = [sb(f"BAT{i}", [128, 32], F32) for i in range(2)]
        BETA = [sb(f"BETA{i}", [128, 16], F32) for i in range(2)]
        GTM = [sb(f"GTM{i}", [128, 16], F32) for i in range(2)]
        GCS = [sb(f"GCS{i}", [128, 16], F32) for i in range(2)]
        AV2 = [sb(f"AV2{i}", [128, 16], F32) for i in range(2)]
        BVA = [sb(f"BVA{i}", [128, 16], F32) for i in range(2)]
        LNRK = [sb(f"LNRK{i}", [128, 8], F32) for i in range(2)]
        LNB = [sb(f"LNB{i}", [128, 16], F32) for i in range(2)]
        EGLB = [sb(f"EGLB{i}", [128, 16], F32) for i in range(2)]
        GTB = [sb(f"GTB{i}", [128, 16], F32) for i in range(2)]
        KSQ = [sb(f"KSQ{i}", [128, 8], F32) for i in range(2)]
        GROWS = [sb(f"GROWS{i}", [16, 128], F32) for i in range(2)]
        LNR8 = [sb(f"LNR8{i}", [8, 128], F32) for i in range(2)]
        QLR = [sb(f"QLR{i}", [1, NKH * 128], F32) for i in range(2)]
        GRW = [sb(f"GRW{i}", [1, NVH * 128], F32) for i in range(2)]
        LRW = [sb(f"LRW{i}", [1, NVH * 128], F32) for i in range(2)]
        BRW = [sb(f"BRW{i}", [1, NVH * 128], F32) for i in range(2)]
        QRW = [sb(f"QRW{i}", [1, NVH * 128], F32) for i in range(2)]
        EQR = [sb(f"EQR{i}", [1, NVH * 128], F32) for i in range(2)]
        EGT = [sb(f"EGT{i}", [1, 16], F32) for i in range(2)]
        GTOT = [sb(f"GTOT{i}", [1, 16], F32) for i in range(2)]
        KTR = [sb(f"KTR{i}", [128, NKH * 128], BF16) for i in range(2)]
        ATM = [sb(f"ATM{i}", [128, NKH * 128], BF16) for i in range(2)]
        K2SL = [sb(f"K2SL{i}", [128, NKH * 128], BF16) for i in range(2)]
        K2SU = [sb(f"K2SU{i}", [128, NKH * 128], BF16) for i in range(2)]
        QSQF = sb("QSQF", [128, 128], F32)
        SCMN = [sb(f"SCMN{i}", [128, 128], F32) for i in range(2)]
        SCMA = [sb(f"SCMA{i}", [128, 128], F32) for i in range(2)]
        EXN = [sb(f"EXN{i}", [128, 128], BF16) for i in range(2)]
        EXPAT = [sb(f"EXPAT{i}", [128, 128], BF16) for i in range(2)]
        NT_ = [sb(f"NT{i}", [128, 128], BF16) for i in range(2)]
        UT_ = [sb(f"UT{i}", [128, 128], BF16) for i in range(2)]
        ATT = [sb(f"ATT{i}", [128, 128], BF16) for i in range(2)]
        QEG = [sb(f"QEG{i}", [128, 128], BF16) for i in range(2)]
        NP_ = [sb(f"NP{i}", [128, 128], BF16) for i in range(6)]
        UP_ = [sb(f"UP{i}", [128, 128], BF16) for i in range(5)]
        PP = [sb(f"PP{i}", [128, 128], BF16) for i in range(2)]
        GIN = [sb(f"GIN{i}", [128, 128], BF16) for i in range(2)]
        KBET = [sb(f"KBET{i}", [128, 128], BF16) for i in range(2)]
        KDEC = [sb(f"KDEC{i}", [128, 128], BF16) for i in range(2)]
        NKC = [sb(f"NKC{i}", [128, 128], BF16) for i in range(2)]
        VBT = [sb(f"VBT{i}", [128, 128], BF16) for i in range(2)]
        VNT = [sb(f"VNT{i}", [128, 128], BF16) for i in range(2)]
        TCOL = [sb(f"TCOL{i}", [128, 1], F32) for i in range(2)]
        KBE = [sb(f"KBE{i}", [128, 1], F32) for i in range(2)]
        KDC = [sb(f"KDC{i}", [128, 1], F32) for i in range(2)]
        OSQ = [sb(f"OSQ{i}", [128, 1], F32) for i in range(2)]
        RSTD = [sb(f"RSTD{i}", [128, 1], F32) for i in range(2)]
        GT1 = [sb(f"GT1{i}", [128, 128], F32) for i in range(2)]
        GT = [sb(f"GTx{i}", [128, D], BF16) for i in range(2)]
        SSB = sb("SSB", [128, NVH * 128], F32)
        SSBB = sb("SSBB", [128, NVH * 128], BF16)

        sems = {e: semh("sem" + e) for e in OpList.SEMS}
        sVCh = semh("sVCh"); sCCh = semh("sCCh")
        _vcc = [0]; _ccc = [0]

        def chv(E, inst):
            _vcc[0] += 1
            inst.then_inc(sVCh, 1)
            E.wait_ge(sVCh, _vcc[0])

        def chc(E, inst):
            _ccc[0] += 1
            inst.then_inc(sCCh, 1)
            E.wait_ge(sCCh, _ccc[0])

        L = OpList()
        A = AluOpType

        fstate = {"F": 0, "B": 0}
        fcons = {"F": [], "B": []}

        def txn(cls):
            i = fstate[cls]
            depth = 6 if cls == "F" else 2
            bank = (FBK if cls == "F" else BBK)[i % depth]
            prev = fcons[cls][i - depth] if i >= depth else None
            fstate[cls] = i + 1
            return bank, prev

        def done(cls, name):
            fcons[cls].append(name)

        def c_loads(E):
            return [E.dma_start(out=IDT[:, :], in_=identb),
                    E.dma_start(out=IDTF[:, :], in_=identf),
                    E.dma_start(out=MSL[:, :], in_=msl),
                    E.dma_start(out=MSU[:, :], in_=msu),
                    E.dma_start(out=MUO[:, :], in_=muo),
                    E.dma_start(out=NWB[:, :], in_=nw),
                    E.dma_start(out=ADT[:, :], in_=adt)]
        L.add("Y", c_loads, name="consts", inc=7 * 16)

        def c_memsets(E):
            E.memset(C30[:, :], 30.0)
            E.memset(ONE1F[:, :], 1.0)
            E.memset(ONEK[:, :], 1.0)
            E.memset(ONEC[:, :], 1.0)
            E.memset(EPS6[:, :], 1e-6)
            E.memset(EPS6C[:, :], 1e-6)
            E.memset(LNK[:, :], -0.5 * math.log(128.0))
            E.memset(SSB[:, :], 0.0)
            return E.memset(SSBB[:, :], 0.0)
        L.add("G", c_memsets, name="memsets", inc=1, sem="H")

        def c_exa(E):
            return E.activation(EXA[:, :], ADT[:, 16:32], AF.Exp)
        L.add("C", c_exa, waits=("consts",), name="exa")

        def prevvh(c, vh, k=2):
            t = c * NVH + vh - k
            if t < 0:
                return None
            return (t // NVH, t % NVH)

        for c in range(nch):
            cc = c % 2

            stw = f"st{c-2}" if c >= 2 else None
            def ld_qkv(E, c=c, cc=cc):
                return E.dma_start(
                    out=QKV[cc][:, :].rearrange("p (n t) -> p n t", n=NQE),
                    in_=qc[:, c * CH:(c + 1) * CH].rearrange("(n p) t -> p n t", p=128))
            L.add("Y", ld_qkv, waits=("consts", stw), name=f"qkv{c}", inc=16)
            def ld_zst(E, c=c, cc=cc):
                return E.dma_start_transpose(ZSC[cc][:, :], zsf[:, c * CH:(c + 1) * CH])
            L.add("Y", ld_zst, waits=(stw,), name=f"zs{c}", inc=16)
            def ld_ba(E, c=c, cc=cc):
                return E.dma_start(out=BAT[cc][:, :], in_=ba[c * CH:(c + 1) * CH, :])
            L.add("Y", ld_ba, waits=(stw,), name=f"ba{c}", inc=16)

            L.add("C", lambda E, cc=cc: E.activation(BETA[cc][:, :], BAT[cc][:, 0:16],
                                                     AF.Exp, scale=-1.0),
                  waits=(f"ba{c}", "exa", f"vbt{c-2}_{NVH-1}" if c >= 2 else None),
                  name=f"sg1{c}")
            def p_sig2(E, cc=cc):
                chv(E, E.tensor_scalar(BETA[cc][:, :], BETA[cc][:, :], 1.0, None, op0=A.add))
                return E.reciprocal(BETA[cc][:, :], BETA[cc][:, :])
            L.add("V", p_sig2, waits=(f"sg1{c}",), name=f"sg2{c}", inc=2)
            L.add("C", lambda E, cc=cc: E.activation(LNB[cc][:, :], BETA[cc][:, :], AF.Ln),
                  waits=(f"sg2{c}",), name=f"lnb{c}")
            L.add("V", lambda E, cc=cc: E.tensor_tensor(GTM[cc][:, :], BAT[cc][:, 16:32],
                                                        ADT[:, 0:16], op=A.add),
                  waits=(f"ba{c}", "exa", f"gcs{c-2}" if c >= 2 else None),
                  name=f"spx{c}")
            def p_spa(E, cc=cc):
                chc(E, E.activation(SPS[:, 0:16], GTM[cc][:, :], AF.Abs))
                chc(E, E.activation(SPS[:, 16:32], SPS[:, 0:16], AF.Exp, scale=-1.0))
                E.activation(SPS[:, 32:48], SPS[:, 16:32], AF.Ln, bias=ONEC[:, 0:1])
                return E.activation(SPS[:, 48:64], GTM[cc][:, :], AF.Relu)
            L.add("C", p_spa, waits=(f"spx{c}", f"g{c-1}" if c >= 1 else None),
                  name=f"spa{c}", inc=4)
            def p_g(E, cc=cc):
                chv(E, E.tensor_tensor(GTM[cc][:, :], SPS[:, 32:48], SPS[:, 48:64], op=A.add))
                return E.scalar_tensor_tensor(GTM[cc][:, :], GTM[cc][:, :], -1.0, EXA[:, :],
                                              op0=A.mult, op1=A.mult)
            L.add("V", p_g, waits=(f"spa{c}",), name=f"g{c}", inc=2)

            bank, rw = txn("F")
            L.add("T", lambda E, cc=cc, bank=bank: E.matmul(bank[:, 0:16], MUO[:, :],
                                                            GTM[cc][:, :], start=True, stop=True),
                  waits=(f"g{c}", "memsets", rw), name=f"gcs{c}")
            L.add("V", lambda E, cc=cc, bank=bank: E.tensor_copy(GCS[cc][:, :], bank[:, 0:16]),
                  waits=(f"gcs{c}",), name=f"gcsev{c}")
            done("F", f"gcsev{c}")
            bank, rw = txn("F")
            L.add("T", lambda E, cc=cc, bank=bank: E.matmul(bank[0:1, 0:16], ONEK[:, :],
                                                            GTM[cc][:, :], start=True, stop=True),
                  waits=(f"g{c}", rw), name=f"gtot{c}")
            def c_egt(E, cc=cc, bank=bank):
                E.activation(EGT[cc][:, :], bank[0:1, 0:16], AF.Exp)
                return E.activation(GTOT[cc][:, :], bank[0:1, 0:16], AF.Identity)
            L.add("C", c_egt, waits=(f"gtot{c}", f"bc1{c-2}" if c >= 2 else None),
                  name=f"egt{c}", inc=2)
            done("F", f"egt{c}")
            bank, rw = txn("F")
            L.add("T", lambda E, cc=cc, bank=bank: E.matmul(bank[0:16, 0:128], GCS[cc][:, :],
                                                            IDTF[:, :], start=True, stop=True),
                  waits=(f"gcsev{c}", rw), name=f"rows{c}")
            L.add("V", lambda E, cc=cc, bank=bank: E.tensor_copy(GROWS[cc][:, :], bank[0:16, 0:128]),
                  waits=(f"rows{c}", f"fl1{c-2}" if c >= 2 else None), name=f"rowsev{c}")
            done("F", f"rowsev{c}")
            bank, rw = txn("F")
            def t_bc1(E, cc=cc, bank=bank):
                E.matmul(bank[:, 0:16], ONE1F[:, :], EGT[cc][:, :], start=True, stop=True)
                return E.matmul(bank[:, 16:32], ONE1F[:, :], GTOT[cc][:, :], start=True, stop=True)
            L.add("T", t_bc1, waits=(f"egt{c}", rw), name=f"bc1{c}", inc=1)
            def v_bc1(E, cc=cc, bank=bank):
                E.tensor_copy(EGLB[cc][:, :], bank[:, 0:16])
                return E.tensor_copy(GTB[cc][:, :], bank[:, 16:32])
            L.add("V", v_bc1, waits=(f"bc1{c}", f"sst{c-1}_{NVH-1}" if c >= 1 else None,
                                     f"tdk{c-1}_{NVH-1}" if c >= 1 else None),
                  name=f"bc1ev{c}", inc=2)
            done("F", f"bc1ev{c}")

            for kh in range(NKH):
                ktile = 8 + kh
                bank, rw = txn("B")
                L.add("T", lambda E, cc=cc, ktile=ktile, bank=bank:
                      E.transpose(bank[:, 0:128], QKV[cc][:, ktile * CH:(ktile + 1) * CH], IDT[:, :]),
                      waits=(f"qkv{c}", "consts", rw), name=f"ktr{c}_{kh}")
                L.add("V", lambda E, cc=cc, kh=kh, bank=bank:
                      E.tensor_copy(KTR[cc][:, kh * 128:(kh + 1) * 128], bank[:, 0:128]),
                      waits=(f"ktr{c}_{kh}",), name=f"ktrev{c}_{kh}")
                done("B", f"ktrev{c}_{kh}")
                L.add("C", lambda E, cc=cc, kh=kh:
                      E.activation(TRASHB[:, :], KTR[cc][:, kh * 128:(kh + 1) * 128],
                                   AF.Square, accum_out=KSQ[cc][:, kh:kh + 1]),
                      waits=(f"ktrev{c}_{kh}",), name=f"ksq{c}_{kh}")
                L.add("C", lambda E, cc=cc, kh=kh:
                      E.activation(LNRK[cc][:, kh:kh + 1], KSQ[cc][:, kh:kh + 1],
                                   AF.Ln, bias=EPS6C[:, 0:1]),
                      waits=(f"ksq{c}_{kh}",), name=f"clnk{c}_{kh}")
                L.add("V", lambda E, cc=cc, kh=kh:
                      E.tensor_scalar(LNRK[cc][:, kh:kh + 1], LNRK[cc][:, kh:kh + 1],
                                      -0.5, None, op0=A.mult),
                      waits=(f"clnk{c}_{kh}",), name=f"lnk{c}_{kh}")
                L.add("C", lambda E, cc=cc, kh=kh:
                      E.activation(QSQF[:, :], QKV[cc][:, kh * CH:(kh + 1) * CH], AF.Square),
                      waits=(f"qkv{c}", f"qssq{c}_{kh-1}" if kh >= 1 else
                             (f"qssq{c-1}_{NKH-1}" if c >= 1 else None)),
                      name=f"qsq{c}_{kh}")
                bank, rw = txn("F")
                L.add("T", lambda E, bank=bank: E.matmul(bank[0:1, 0:128], ONEK[:, :],
                                                         QSQF[:, :], start=True, stop=True),
                      waits=(f"qsq{c}_{kh}", rw), name=f"qssq{c}_{kh}")
                L.add("C", lambda E, cc=cc, kh=kh, bank=bank:
                      E.activation(QLR[cc][0:1, kh * 128:(kh + 1) * 128], bank[0:1, 0:128],
                                   AF.Ln, bias=EPS6[0:1, 0:1]),
                      waits=(f"qssq{c}_{kh}", f"qrw{c-2}" if c >= 2 else None),
                      name=f"qrow{c}_{kh}")
                done("F", f"qrow{c}_{kh}")
                bank, rw = txn("F")
                def t_atr(E, cc=cc, kh=kh, ktile=ktile, bank=bank):
                    E.matmul(bank[:, 0:128], QKV[cc][:, ktile * CH:(ktile + 1) * CH],
                             QKV[cc][:, kh * CH:(kh + 1) * CH], start=True, stop=True)
                    return E.matmul(bank[:, 128:256], QKV[cc][:, ktile * CH:(ktile + 1) * CH],
                                    QKV[cc][:, ktile * CH:(ktile + 1) * CH], start=True, stop=True)
                L.add("T", t_atr, waits=(f"qkv{c}", rw), name=f"atr{c}_{kh}", inc=1)
                def v_k2(E, cc=cc, kh=kh, bank=bank):
                    E.tensor_tensor(ATM[cc][:, kh * 128:(kh + 1) * 128], bank[:, 0:128],
                                    MUO[:, :], op=A.mult)
                    E.tensor_tensor(K2SL[cc][:, kh * 128:(kh + 1) * 128], bank[:, 128:256],
                                    MSL[:, :], op=A.mult)
                    return E.tensor_tensor(K2SU[cc][:, kh * 128:(kh + 1) * 128], bank[:, 128:256],
                                           MSU[:, :], op=A.mult)
                L.add("V", v_k2, waits=(f"atr{c}_{kh}", "consts"), name=f"k2ev{c}_{kh}", inc=3)
                done("F", f"k2ev{c}_{kh}")

            bank, rw = txn("F")
            L.add("T", lambda E, cc=cc, bank=bank: E.matmul(bank[0:8, 0:128], LNRK[cc][:, :],
                                                            IDTF[:, :], start=True, stop=True),
                  waits=tuple(f"lnk{c}_{k}" for k in range(NKH)) + (rw,), name=f"lnr{c}")
            L.add("V", lambda E, cc=cc, bank=bank: E.tensor_copy(LNR8[cc][:, :], bank[0:8, 0:128]),
                  waits=(f"lnr{c}", f"fl2{c-2}" if c >= 2 else None), name=f"lnrev{c}")
            done("F", f"lnrev{c}")

            L.add("G", lambda E, cc=cc: E.dma_start(out=rb16, in_=GROWS[cc][:, :]),
                  waits=(f"rowsev{c}", f"fl1{c-1}" if c >= 1 else None),
                  name=f"fl1a{c}", inc=16)
            L.add("G", lambda E, cc=cc: E.dma_start(
                out=GRW[cc][0:1, :], in_=rb16.rearrange("h d -> (h d)").rearrange("(o f) -> o f", o=1)),
                waits=(f"fl1a{c}", f"brw{c-2}" if c >= 2 else None, f"qrw{c-2}" if c >= 2 else None),
                name=f"fl1{c}", inc=16)
            L.add("G", lambda E, cc=cc: E.dma_start(out=rb8, in_=LNR8[cc][:, :]),
                  waits=(f"lnrev{c}", f"fl2{c-1}" if c >= 1 else None),
                  name=f"fl2a{c}", inc=16)
            L.add("G", lambda E, cc=cc: E.dma_start(
                out=LRW[cc][0:1, 0:NKH * 128],
                in_=rb8.rearrange("h d -> (h d)").rearrange("(o f) -> o f", o=1)),
                waits=(f"fl2a{c}", f"brw{c-2}" if c >= 2 else None), name=f"fl2{c}", inc=16)

            def v_brw(E, cc=cc):
                lv = LRW[cc][0:1, 0:NKH * 128].rearrange("o (h d) -> o h d", d=128)
                bv = BRW[cc][0:1, :].rearrange("o (h two d) -> o h two d", two=2, d=128)
                chv(E, E.tensor_copy(bv[:, :, 0, :], lv))
                chv(E, E.tensor_copy(bv[:, :, 1, :], lv))
                return E.tensor_tensor(BRW[cc][0:1, :], BRW[cc][0:1, :], GRW[cc][0:1, :],
                                       op=A.subtract)
            L.add("V", v_brw, waits=(f"fl1{c}", f"fl2{c}",
                                     f"bc{c-2}_{NVH-1}" if c >= 2 else None),
                  name=f"brw{c}", inc=3)
            def v_qrw(E, cc=cc):
                qv_ = QLR[cc][0:1, 0:NKH * 128].rearrange("o (h d) -> o h d", d=128)
                qw = QRW[cc][0:1, :].rearrange("o (h two d) -> o h two d", two=2, d=128)
                chv(E, E.tensor_copy(qw[:, :, 0, :], qv_))
                chv(E, E.tensor_copy(qw[:, :, 1, :], qv_))
                chv(E, E.scalar_tensor_tensor(QRW[cc][0:1, :], QRW[cc][0:1, :], -0.5,
                                              GRW[cc][0:1, :], op0=A.mult, op1=A.add))
                return E.tensor_scalar(QRW[cc][0:1, :], QRW[cc][0:1, :],
                                       -0.5 * math.log(128.0), None, op0=A.add)
            L.add("V", v_qrw, waits=(f"fl1{c}", f"qrow{c}_{NKH-1}",
                                     f"bc{c-2}_{NVH-1}" if c >= 2 else None),
                  name=f"qrw{c}", inc=4)
            L.add("C", lambda E, cc=cc: E.activation(EQR[cc][0:1, :], QRW[cc][0:1, :], AF.Exp),
                  waits=(f"qrw{c}", f"bc{c-2}_{NVH-1}" if c >= 2 else None), name=f"eqr{c}")

            def v_av(E, cc=cc):
                lv = LNRK[cc][:, :].rearrange("p (h o) -> p h o", o=1)
                a2 = AV2[cc][:, :].rearrange("p (h two) -> p h two", two=2)
                b2 = BVA[cc][:, :].rearrange("p (h two) -> p h two", two=2)
                chv(E, E.tensor_tensor(AV2[cc][:, :], GCS[cc][:, :], LNB[cc][:, :], op=A.add))
                chv(E, E.tensor_tensor(a2[:, :, 0:1], a2[:, :, 0:1], lv, op=A.add))
                chv(E, E.tensor_tensor(a2[:, :, 1:2], a2[:, :, 1:2], lv, op=A.add))
                chv(E, E.tensor_scalar(BVA[cc][:, :], GCS[cc][:, :], -1.0, None, op0=A.mult))
                chv(E, E.tensor_tensor(b2[:, :, 0:1], b2[:, :, 0:1], lv, op=A.add))
                return E.tensor_tensor(b2[:, :, 1:2], b2[:, :, 1:2], lv, op=A.add)
            L.add("V", v_av, waits=(f"gcsev{c}", f"lnb{c}") +
                  tuple(f"lnk{c}_{k}" for k in range(NKH)) +
                  ((f"kbe{c-2}_{NVH-1}", f"tdk{c-2}_{NVH-1}") if c >= 2 else ()),
                  name=f"av{c}", inc=6)

            for vh in range(NVH):
                kh = vh // 2
                vv = vh % 2
                vtile = 16 + vh
                pv = prevvh(c, vh)

                bank, rw = txn("F")
                def t_bc(E, cc=cc, vh=vh, bank=bank):
                    E.matmul(bank[:, 0:128], ONE1F[:, :], BRW[cc][0:1, vh * 128:(vh + 1) * 128],
                             start=True, stop=True)
                    E.matmul(bank[:, 128:256], ONE1F[:, :], QRW[cc][0:1, vh * 128:(vh + 1) * 128],
                             start=True, stop=True)
                    return E.matmul(bank[:, 256:384], ONE1F[:, :],
                                    EQR[cc][0:1, vh * 128:(vh + 1) * 128], start=True, stop=True)
                L.add("T", t_bc, waits=(f"brw{c}", f"qrw{c}", f"eqr{c}", rw), name=f"bc{c}_{vh}", inc=1)
                def v_scm(E, cc=cc, vh=vh, vv=vv, kh=kh, bank=bank):
                    E.scalar_tensor_tensor(SCMN[vv][:, :], bank[:, 0:128], AV2[cc][:, vh:vh + 1],
                                           C30[:, :], op0=A.add, op1=A.min)
                    E.scalar_tensor_tensor(SCMA[vv][:, :], bank[:, 128:256], BVA[cc][:, vh:vh + 1],
                                           C30[:, :], op0=A.add, op1=A.min)
                    return E.tensor_tensor(QEG[vv][:, :], QKV[cc][:, kh * CH:(kh + 1) * CH],
                                           bank[:, 256:384], op=A.mult)
                L.add("V", v_scm, waits=(f"bc{c}_{vh}", f"av{c}",
                                         f"exn{pv[0]}_{pv[1]}" if pv else None,
                                         f"tout{pv[0]}_{pv[1]}" if pv else None),
                      name=f"scm{c}_{vh}", inc=3)
                done("F", f"scm{c}_{vh}")
                def c_exn(E, vv=vv):
                    E.activation(EXN[vv][:, :], SCMN[vv][:, :], AF.Exp)
                    return E.activation(EXPAT[vv][:, :], SCMA[vv][:, :], AF.Exp)
                L.add("C", c_exn, waits=(f"scm{c}_{vh}",
                                         f"nuev{pv[0]}_{pv[1]}" if pv else None,
                                         f"tru{pv[0]}_{pv[1]}" if pv else None),
                      name=f"exn{c}_{vh}", inc=2)
                bank, rw = txn("B")
                L.add("T", lambda E, vv=vv, bank=bank:
                      E.transpose(bank[:, 0:128], EXN[vv][:, :], IDT[:, :]),
                      waits=(f"exn{c}_{vh}", rw), name=f"tru{c}_{vh}")
                def v_nu(E, cc=cc, vv=vv, kh=kh, bank=bank):
                    E.tensor_tensor(NT_[vv][:, :], K2SL[cc][:, kh * 128:(kh + 1) * 128],
                                    EXN[vv][:, :], op=A.mult)
                    E.tensor_tensor(UT_[vv][:, :], K2SU[cc][:, kh * 128:(kh + 1) * 128],
                                    bank[:, 0:128], op=A.mult)
                    return E.tensor_tensor(ATT[vv][:, :], ATM[cc][:, kh * 128:(kh + 1) * 128],
                                           EXPAT[vv][:, :], op=A.mult)
                L.add("V", v_nu, waits=(f"tru{c}_{vh}", f"exn{c}_{vh}", f"k2ev{c}_{kh}",
                                        f"sq{pv[0]}_{pv[1]}_1" if pv else None,
                                        f"tout{pv[0]}_{pv[1]}" if pv else None),
                      name=f"nuev{c}_{vh}", inc=3)
                done("B", f"nuev{c}_{vh}")
                pv1 = prevvh(c, vh, 1)
                L.add("V", lambda E, vv=vv: E.tensor_tensor(PP[0][:, :], IDT[:, :], UT_[vv][:, :],
                                                            op=A.subtract),
                      waits=(f"nuev{c}_{vh}", "consts",
                             f"gfin{pv1[0]}_{pv1[1]}" if pv1 else None),
                      name=f"p0{c}_{vh}")
                chain_ops = [("N2", "U", "N"), ("U2", "N", "U"), ("N4", "U2", "N2"),
                             ("U4", "N2", "U2"), ("N8", "U4", "N4"), ("U8", "N4", "U4"),
                             ("N16", "U8", "N8"), ("U16", "N8", "U8"), ("N32", "U16", "N16"),
                             ("U32", "N16", "U16"), ("N64", "U32", "N32")]
                npow = {"N2": NP_[0], "N4": NP_[1], "N8": NP_[2], "N16": NP_[3], "N32": NP_[4],
                        "N64": NP_[5], "U2": UP_[0], "U4": UP_[1], "U8": UP_[2], "U16": UP_[3],
                        "U32": UP_[4]}
                evac_of = {}
                def get_tile(nm, vv=vv):
                    if nm == "N":
                        return NT_[vv][:, :]
                    if nm == "U":
                        return UT_[vv][:, :]
                    return npow[nm][:, :]
                for ci_, (onm, lnm, rnm) in enumerate(chain_ops):
                    bank, rw = txn("F")
                    opwaits = [f"nuev{c}_{vh}", rw]
                    for nm in (lnm, rnm):
                        if nm in evac_of:
                            opwaits.append(evac_of[nm])
                    if pv1:
                        opwaits.append(f"gfin{pv1[0]}_{pv1[1]}")
                    def t_sq(E, lnm=lnm, rnm=rnm, bank=bank, vv=vv):
                        return E.matmul(bank[:, 0:128], get_tile(lnm, vv), get_tile(rnm, vv),
                                        start=True, stop=True)
                    L.add("T", t_sq, waits=tuple(opwaits), name=f"sq{c}_{vh}_{ci_}")
                    def v_sqe(E, onm=onm, bank=bank):
                        return E.tensor_copy(npow[onm][:, :], bank[:, 0:128])
                    L.add("V", v_sqe, waits=(f"sq{c}_{vh}_{ci_}",), name=f"sqe{c}_{vh}_{ci_}")
                    done("F", f"sqe{c}_{vh}_{ci_}")
                    evac_of[onm] = f"sqe{c}_{vh}_{ci_}"
                for pi_, lnm in enumerate(["N2", "N4", "N8", "N16", "N32", "N64"]):
                    bank, rw = txn("F")
                    w = [evac_of[lnm], (f"p0{c}_{vh}" if pi_ == 0 else f"pre{c}_{vh}_{pi_-1}"), rw]
                    def t_pr(E, lnm=lnm, pi_=pi_, bank=bank, vv=vv):
                        return E.matmul(bank[:, 0:128], get_tile(lnm, vv), PP[pi_ % 2][:, :],
                                        start=True, stop=True)
                    L.add("T", t_pr, waits=tuple(w), name=f"pr{c}_{vh}_{pi_}")
                    def v_pre(E, pi_=pi_, bank=bank):
                        return E.tensor_tensor(PP[(pi_ + 1) % 2][:, :], PP[pi_ % 2][:, :],
                                               bank[:, 0:128], op=A.add)
                    L.add("V", v_pre, waits=(f"pr{c}_{vh}_{pi_}",), name=f"pre{c}_{vh}_{pi_}")
                    done("F", f"pre{c}_{vh}_{pi_}")
                def v_gf(E, vh=vh):
                    return E.tensor_copy(GIN[vh % 2][:, :], PP[0][:, :])
                L.add("V", v_gf, waits=(f"pre{c}_{vh}_5",
                                        f"tkc{pv[0]}_{pv[1]}" if pv else None,
                                        f"tvn{pv[0]}_{pv[1]}" if pv else None),
                      name=f"gfin{c}_{vh}")
                L.add("C", lambda E, cc=cc, vh=vh, vv=vv:
                      E.activation(KBE[vv][:, :], AV2[cc][:, vh:vh + 1], AF.Exp),
                      waits=(f"av{c}", f"kk{pv[0]}_{pv[1]}" if pv else None),
                      name=f"kbe{c}_{vh}")
                L.add("V", lambda E, cc=cc, vh=vh, vv=vv:
                      E.tensor_tensor(TCOL[vv][:, :], BVA[cc][:, vh:vh + 1],
                                      GTB[cc][:, vh:vh + 1], op=A.add),
                      waits=(f"av{c}", f"bc1ev{c}", f"kdc{pv[0]}_{pv[1]}" if pv else None),
                      name=f"tdk{c}_{vh}")
                L.add("C", lambda E, vv=vv: E.activation(KDC[vv][:, :], TCOL[vv][:, :], AF.Exp),
                      waits=(f"tdk{c}_{vh}", f"kk{pv[0]}_{pv[1]}" if pv else None),
                      name=f"kdc{c}_{vh}")
                def v_kk(E, cc=cc, vh=vh, vv=vv, kh=kh):
                    E.tensor_scalar(KBET[vv][:, :], KTR[cc][:, kh * 128:(kh + 1) * 128],
                                    KBE[vv][:, 0:1], None, op0=A.mult)
                    return E.tensor_scalar(KDEC[vv][:, :], KTR[cc][:, kh * 128:(kh + 1) * 128],
                                           KDC[vv][:, 0:1], None, op0=A.mult)
                L.add("V", v_kk, waits=(f"kbe{c}_{vh}", f"kdc{c}_{vh}", f"ktrev{c}_{kh}",
                                        f"tkc{pv[0]}_{pv[1]}" if pv else None,
                                        f"tds{pv[0]}_{pv[1]}" if pv else None),
                      name=f"kk{c}_{vh}", inc=2)
                bank, rw = txn("F")
                L.add("T", lambda E, vh=vh, vv=vv, bank=bank:
                      E.matmul(bank[:, 0:128], KBET[vv][:, :], GIN[vh % 2][:, :],
                               start=True, stop=True),
                      waits=(f"kk{c}_{vh}", f"gfin{c}_{vh}", rw), name=f"tkc{c}_{vh}")
                L.add("V", lambda E, vv=vv, bank=bank:
                      E.tensor_scalar(NKC[vv][:, :], bank[:, 0:128], -1.0, None, op0=A.mult),
                      waits=(f"tkc{c}_{vh}", f"tvn{pv[0]}_{pv[1]}" if pv else None),
                      name=f"nkc{c}_{vh}")
                done("F", f"nkc{c}_{vh}")
                bank, rw = txn("B")
                L.add("T", lambda E, cc=cc, vtile=vtile, bank=bank:
                      E.transpose(bank[:, 0:128], QKV[cc][:, vtile * CH:(vtile + 1) * CH], IDT[:, :]),
                      waits=(f"qkv{c}", rw), name=f"vtr{c}_{vh}")
                L.add("V", lambda E, cc=cc, vh=vh, vv=vv, bank=bank:
                      E.tensor_scalar(VBT[vv][:, :], bank[:, 0:128],
                                      BETA[cc][:, vh:vh + 1], None, op0=A.mult),
                      waits=(f"vtr{c}_{vh}", f"sg2{c}",
                             f"tvn{pv[0]}_{pv[1]}" if pv else None),
                      name=f"vbt{c}_{vh}")
                done("B", f"vbt{c}_{vh}")
                bank, rw = txn("F")
                svw = f"sst{c-1}_{vh}" if c >= 1 else "memsets"
                def t_vn(E, vh=vh, vv=vv, bank=bank):
                    E.matmul(bank[:, 0:128], GIN[vh % 2][:, :], VBT[vv][:, :],
                             start=True, stop=False)
                    return E.matmul(bank[:, 0:128], NKC[vv][:, :],
                                    SSBB[:, vh * 128:(vh + 1) * 128], start=False, stop=True)
                L.add("T", t_vn, waits=(f"gfin{c}_{vh}", f"vbt{c}_{vh}", f"nkc{c}_{vh}", svw, rw),
                      name=f"tvn{c}_{vh}", inc=1)
                L.add("V", lambda E, vv=vv, bank=bank: E.tensor_copy(VNT[vv][:, :], bank[:, 0:128]),
                      waits=(f"tvn{c}_{vh}", f"tds{pv[0]}_{pv[1]}" if pv else None,
                             f"tout{pv[0]}_{pv[1]}" if pv else None),
                      name=f"vnt{c}_{vh}")
                done("F", f"vnt{c}_{vh}")
                bank, rw = txn("F")
                def t_out(E, vh=vh, vv=vv, bank=bank):
                    E.matmul(bank[:, 0:128], QEG[vv][:, :], SSBB[:, vh * 128:(vh + 1) * 128],
                             start=True, stop=False)
                    return E.matmul(bank[:, 0:128], ATT[vv][:, :], VNT[vv][:, :],
                                    start=False, stop=True)
                L.add("T", t_out, waits=(f"scm{c}_{vh}", f"nuev{c}_{vh}", f"vnt{c}_{vh}", svw, rw),
                      name=f"tout{c}_{vh}", inc=1)
                def c_osq(E, vv=vv, bank=bank):
                    chc(E, E.activation(TRASHB[:, :], bank[:, 0:128], AF.Square,
                                        accum_out=OSQ[vv][:, 0:1]))
                    chc(E, E.activation(RSTD[vv][:, :], OSQ[vv][:, 0:1], AF.Ln,
                                        scale=1.0 / 128.0, bias=EPS6C[:, 0:1]))
                    return E.activation(RSTD[vv][:, :], RSTD[vv][:, :], AF.Exp, scale=-0.5)
                L.add("C", c_osq, waits=(f"tout{c}_{vh}",
                                         f"gt{pv[0]}_{pv[1]}" if pv else None),
                      name=f"osq{c}_{vh}", inc=3)
                def v_gt(E, cc=cc, vh=vh, vv=vv, bank=bank):
                    chv(E, E.scalar_tensor_tensor(GT1[vv][:, :], bank[:, 0:128], RSTD[vv][:, 0:1],
                                                  ZSC[cc][:, vh * 128:(vh + 1) * 128],
                                                  op0=A.mult, op1=A.mult))
                    return E.tensor_tensor(GT[cc][:, vh * 128:(vh + 1) * 128], GT1[vv][:, :],
                                           NWB[:, :], op=A.mult)
                L.add("V", v_gt, waits=(f"osq{c}_{vh}", f"zs{c}",
                                        f"st{c-2}" if c >= 2 else None),
                      name=f"gt{c}_{vh}", inc=2)
                done("F", f"gt{c}_{vh}")
                bank, rw = txn("F")
                L.add("T", lambda E, vv=vv, bank=bank:
                      E.matmul(bank[:, 0:128], KDEC[vv][:, :], VNT[vv][:, :], start=True, stop=True),
                      waits=(f"kk{c}_{vh}", f"vnt{c}_{vh}", svw, rw), name=f"tds{c}_{vh}")
                def v_sst(E, cc=cc, vh=vh, bank=bank):
                    chv(E, E.scalar_tensor_tensor(SSB[:, vh * 128:(vh + 1) * 128],
                                                  SSB[:, vh * 128:(vh + 1) * 128],
                                                  EGLB[cc][:, vh:vh + 1], bank[:, 0:128],
                                                  op0=A.mult, op1=A.add))
                    return E.tensor_copy(SSBB[:, vh * 128:(vh + 1) * 128],
                                         SSB[:, vh * 128:(vh + 1) * 128])
                L.add("V", v_sst, waits=(f"tds{c}_{vh}", f"bc1ev{c}"), name=f"sst{c}_{vh}", inc=2)
                done("F", f"sst{c}_{vh}")

            L.add("G", lambda E, cc=cc, c=c: E.dma_start(out=gated[c * CH:(c + 1) * CH, :],
                                                         in_=GT[cc][:, :]),
                  waits=tuple(f"gt{c}_{vh}" for vh in range(NVH)), name=f"st{c}", inc=16)

        if debug:
            dbg_names = [("GCS", GCS[0]), ("AV2", AV2[0]), ("BVA", BVA[0]), ("BETA", BETA[0]),
                         ("GTM", GTM[0]), ("LNRK", LNRK[0]), ("GRW", GRW[0]), ("QRW", QRW[0]),
                         ("BRW", BRW[0]), ("EQR", EQR[0]), ("KTR", KTR[0]), ("SSB", SSB),
                         ("ZSC", ZSC[0]), ("EGLB", EGLB[0]), ("GTB", GTB[0]), ("QLR", QLR[0]),
                         ("GIN0", GIN[0]), ("GIN1", GIN[1]), ("VNT0", VNT[0]), ("VNT1", VNT[1]),
                         ("NT0", NT_[0]), ("NT1", NT_[1]), ("UT0", UT_[0]), ("UT1", UT_[1]),
                         ("KBET0", KBET[0]), ("KBET1", KBET[1]), ("KDEC0", KDEC[0]),
                         ("KDEC1", KDEC[1]), ("VBT0", VBT[0]), ("VBT1", VBT[1]),
                         ("N64", NP_[5]), ("U32", UP_[4]), ("EXN0", EXN[0]), ("EXN1", EXN[1]),
                         ("OSQ0", OSQ[0]), ("OSQ1", OSQ[1]), ("RSTD0", RSTD[0]), ("RSTD1", RSTD[1]),
                         ("ATT0", ATT[0]), ("ATT1", ATT[1]), ("GT", GT[0])]
            dbg_outs = {}
            for nm, t in dbg_names:
                shp = list(t.shape)
                dt = t.dtype
                dbg_outs[nm] = nc.dram_tensor("dbg_" + nm, shp, dt, kind="ExternalOutput").ap()
            def dbg_dump(E):
                return [E.dma_start(out=dbg_outs[nm], in_=t[:, :]) for nm, t in dbg_names]
            L.add("Y", dbg_dump, waits=(f"st{nch-1}",), name="dbgd", inc=16 * len(dbg_names))

        def fin_sync(E):
            E.wait_ge(sems["G"], L.evt[f"st{nch-1}"][1])
            if debug:
                E.wait_ge(sems["Y"], L.evt["dbgd"][1])
        block.sync(lambda E: L.emit("Y", E, sems))
        block.gpsimd(lambda E: L.emit("G", E, sems))
        block.tensor(lambda E: L.emit("T", E, sems))
        block.vector(lambda E: L.emit("V", E, sems))
        block.scalar(lambda E: L.emit("C", E, sems))
        block.sync(fin_sync)
    return nc



HFF = 5632
NHE = HFF // 128

def build_l1f(nc, fx=None, stop_after=None, nbd=None):
    fused = fx is not None
    if fused:
        gtm = fx.aps["gtm"]
        x1f32 = fx.aps["x1f32"]
    else:
        gtm = nc.dram_tensor("gtm", [NT, 4096], BF16, kind="ExternalInput").ap()
        x1f32 = nc.dram_tensor("x1f32", [D, NT], F32, kind="ExternalInput").ap()
    wout_et = nc.dram_tensor("wout_et", [NE, 4096, 128], F32, kind="ExternalInput").ap()
    w1t = nc.dram_tensor("w1t", [D, HFF], F32, kind="ExternalInput").ap()
    w2t = nc.dram_tensor("w2t", [D, HFF], F32, kind="ExternalInput").ap()
    w3t = nc.dram_tensor("w3t", [HFF, D], F32, kind="ExternalInput").ap()
    x3o = nc.dram_tensor("x3", [D, NT], F32, kind="ExternalOutput").ap()
    x2f = nc.dram_tensor("x2f", [D, NT], F32,
                         kind="ExternalOutput" if stop_after else "Internal").ap()
    x2b = nc.dram_tensor("x2b", [D, NT], BF16, kind="Internal").ap()
    w1q = nc.dram_tensor("w1q", [D, HFF], BF16, kind="Internal").ap()
    w2q = nc.dram_tensor("w2q", [D, HFF], BF16, kind="Internal").ap()
    w3q = nc.dram_tensor("w3q", [HFF, D], BF16, kind="Internal").ap()

    import contextlib
    es = contextlib.ExitStack()
    with es:
        pfx = fx.prefix if fused else ""
        sb = lambda name, shape, dt: es.enter_context(nc.sbuf_tensor(pfx + name, shape, dt))
        psd = lambda name, shape, dt: es.enter_context(nc.psum_tensor(pfx + name, shape, dt))
        if fused:
            semh = lambda name: fx.es.enter_context(nc.semaphore(pfx + name))
            block = fx.block
        else:
            semh = lambda name: es.enter_context(nc.semaphore(name))
            block = es.enter_context(nc.Block())

        FBK = [psd(f"pF{i}", [128, 512], F32) for i in range(8)]

        ONEK = sb("ONEK", [128, 1], F32)
        ONE1F = sb("ONE1F", [1, 128], F32)
        EPS1 = sb("EPS1", [1, 1], F32)
        EPS8 = sb("EPS8", [1, 1], F32)
        MEAN = sb("MEAN", [1, TB], F32)
        M2 = sb("M2", [1, TB], F32)
        VARV = sb("VARV", [1, TB], F32)
        LNV = sb("LNV", [1, TB], F32)
        RSTD = sb("RSTD", [1, TB], F32)
        NMRS = sb("NMRS", [1, TB], F32)
        ABC = sb("ABC", [128, TB], F32)
        BBC = sb("BBC", [128, TB], F32)
        SQ = [sb(f"SQ{i}", [128, TB], F32) for i in range(2)]
        XR = sb("XR", [128, NE * TB], F32)
        XA = sb("XA", [128, NE * TB], F32)
        XAB = sb("XAB", [128, NE * TB], BF16)
        GAC = sb("GAC", [128, 4], F32)
        INVG = sb("INVG", [128, 4], F32)
        GSC = sb("GSC", [1, 4], F32)

        sems = {e: semh("sem" + e) for e in OpList.SEMS}
        sVCh = semh("sVCh"); sCCh = semh("sCCh")
        _vcc = [0]; _ccc = [0]

        def chv(E, inst):
            _vcc[0] += 1
            inst.then_inc(sVCh, 1)
            E.wait_ge(sVCh, _vcc[0])

        def chc(E, inst):
            _ccc[0] += 1
            inst.then_inc(sCCh, 1)
            E.wait_ge(sCCh, _ccc[0])

        L = OpList()
        A = AluOpType
        rotst = [0]
        rcons = []

        def txn():
            i = rotst[0]
            rotst[0] = i + 1
            return FBK[i % 8], (rcons[i - 8] if i >= 8 else None)

        def done(nm):
            rcons.append(nm)

        pm = {e: 0 for e in OpList.ENG}

        def emit_phase():
            m2 = {e: len(L.ops[e]) for e in OpList.ENG}
            block.sync(lambda E, lo=pm["Y"], hi=m2["Y"]: L.emit_range("Y", E, sems, lo, hi))
            block.gpsimd(lambda E, lo=pm["G"], hi=m2["G"]: L.emit_range("G", E, sems, lo, hi))
            block.tensor(lambda E, lo=pm["T"], hi=m2["T"]: L.emit_range("T", E, sems, lo, hi))
            block.vector(lambda E, lo=pm["V"], hi=m2["V"]: L.emit_range("V", E, sems, lo, hi))
            block.scalar(lambda E, lo=pm["C"], hi=m2["C"]: L.emit_range("C", E, sems, lo, hi))
            pm.update(m2)

        def c_memsets(E):
            E.memset(ONEK[:, :], 1.0)
            E.memset(ONE1F[:, :], 1.0)
            E.memset(EPS1[:, :], LNEPS)
            E.memset(EPS8[:, :], 1e-8)
            return E.memset(GAC[:, :], 0.0)
        L.add("G", c_memsets, name="memsets", inc=1, sem="H")

        ln_id = [0]

        def emit_ln(tag, reswait, sqgate_prev, applygate, xab):
            i = ln_id[0]
            ln_id[0] += 1
            b1, rw1 = txn()
            def t_sum(E, b1=b1):
                last = None
                for e in range(NE):
                    last = E.matmul(b1[0:1, 0:TB], ONEK[:, :], XR[:, e * TB:(e + 1) * TB],
                                    start=(e == 0), stop=(e == NE - 1))
                return last
            L.add("T", t_sum, waits=tuple(reswait(e) for e in range(NE)) + (rw1,),
                  name=f"{tag}sum")
            for e in range(NE):
                L.add("C", lambda E, e=e: E.activation(SQ[e % 2][:, :], XR[:, e * TB:(e + 1) * TB],
                                                       AF.Square),
                      waits=(reswait(e), f"{tag}ssq{e-2}" if e >= 2 else sqgate_prev),
                      name=f"{tag}sq{e}")
            b2, rw2 = txn()
            for e in range(NE):
                L.add("T", lambda E, e=e, b2=b2: E.matmul(b2[0:1, 0:TB], ONEK[:, :], SQ[e % 2][:, :],
                                                          start=(e == 0), stop=(e == NE - 1)),
                      waits=(f"{tag}sq{e}",) + ((rw2,) if e == 0 else ()),
                      name=f"{tag}ssq{e}")
            def v_sm(E, b1=b1, b2=b2):
                chv(E, E.tensor_scalar(MEAN[:, :], b1[0:1, 0:TB], 1.0 / D, None, op0=A.mult))
                chv(E, E.tensor_tensor(M2[:, :], MEAN[:, :], MEAN[:, :], op=A.mult))
                chv(E, E.tensor_scalar(VARV[:, :], b2[0:1, 0:TB], 1.0 / D, None, op0=A.mult))
                return E.tensor_tensor(VARV[:, :], VARV[:, :], M2[:, :], op=A.subtract)
            L.add("V", v_sm, waits=(f"{tag}sum", f"{tag}ssq{NE-1}"), name=f"{tag}sm", inc=4)
            done(f"{tag}sm")
            done(f"{tag}sm")
            def c_rstd(E):
                chc(E, E.activation(LNV[:, :], VARV[:, :], AF.Ln, bias=EPS1[0:1, 0:1]))
                return E.activation(RSTD[:, :], LNV[:, :], AF.Exp, scale=-0.5)
            L.add("C", c_rstd, waits=(f"{tag}sm",), name=f"{tag}rstd", inc=2)
            def v_nm(E):
                return E.scalar_tensor_tensor(NMRS[:, :], MEAN[:, :], -1.0, RSTD[:, :],
                                              op0=A.mult, op1=A.mult)
            L.add("V", v_nm, waits=(f"{tag}rstd",), name=f"{tag}nm")
            b3, rw3 = txn()
            def t_bc1(E, b3=b3):
                return E.matmul(b3[:, 0:TB], ONE1F[:, :], RSTD[:, :], start=True, stop=True)
            L.add("T", t_bc1, waits=(f"{tag}rstd", f"{tag}nm", rw3), name=f"{tag}bc1")
            b4, rw4 = txn()
            def t_bc2(E, b4=b4):
                return E.matmul(b4[:, 0:TB], ONE1F[:, :], NMRS[:, :], start=True, stop=True)
            L.add("T", t_bc2, waits=(f"{tag}nm", rw4), name=f"{tag}bc2")
            def v_bce(E, b3=b3, b4=b4):
                chv(E, E.tensor_copy(ABC[:, :], b3[:, 0:TB]))
                return E.tensor_copy(BBC[:, :], b4[:, 0:TB])
            L.add("V", v_bce, waits=(f"{tag}bc1", f"{tag}bc2", applygate), name=f"{tag}bce", inc=2)
            done(f"{tag}bce")
            done(f"{tag}bce")
            for e in range(NE):
                def v_ap(E, e=e):
                    chv(E, E.tensor_tensor(XA[:, e * TB:(e + 1) * TB], XR[:, e * TB:(e + 1) * TB],
                                           ABC[:, :], op=A.mult))
                    inst = E.tensor_tensor(XA[:, e * TB:(e + 1) * TB], XA[:, e * TB:(e + 1) * TB],
                                           BBC[:, :], op=A.add)
                    if xab:
                        chv(E, inst)
                        inst = E.tensor_copy(XAB[:, e * TB:(e + 1) * TB], XA[:, e * TB:(e + 1) * TB])
                    return inst
                L.add("V", v_ap, waits=(f"{tag}bce",), name=f"{tag}ap{e}",
                      inc=(3 if xab else 2))
            return f"{tag}ap{NE-1}"

        woq = nc.dram_tensor("woq", [NE, 4096, 128], BF16, kind="Internal").ap()
        es1 = contextlib.ExitStack()
        es1.__enter__()
        sb1 = lambda name, shape, dt: es1.enter_context(nc.sbuf_tensor(pfx + name, shape, dt))
        WCF = sb1("WCF", [128, 32 * 128], F32)
        WCB = sb1("WCB", [128, 32 * 128], BF16)
        INB = sb1("INB", [128, 32 * TB], BF16)
        WQS = [sb1(f"WQS{i}", [128, 32 * 128], BF16) for i in range(2)]

        for e in range(NE):
            def d0_ld(E, e=e):
                return E.dma_start(out=WCF[:, :].rearrange("p (n c) -> p n c", n=32),
                                   in_=wout_et[e].rearrange("(n p) c -> p n c", p=128))
            L.add("Y", d0_ld, waits=((f"d0c{e-1}",) if e >= 1 else ()), name=f"d0l{e}", inc=16)
            def d0_cast(E, e=e):
                return E.tensor_copy(WCB[:, :], WCF[:, :])
            L.add("V", d0_cast, waits=(f"d0l{e}",) + ((f"d0s{e-1}",) if e >= 1 else ()),
                  name=f"d0c{e}")
            def d0_st(E, e=e):
                return E.dma_start(out=woq[e].rearrange("(n p) c -> p n c", p=128),
                                   in_=WCB[:, :].rearrange("p (n c) -> p n c", n=32))
            L.add("G", d0_st, waits=(f"d0c{e}",), name=f"d0s{e}", inc=16)

        NBD = nbd if nbd is not None else NB
        for t in range(NBD):
            def ld_ga(E, t=t):
                return [E.dma_start_transpose(INB[:, kc * TB:(kc + 1) * TB],
                                              gtm[t * TB:(t + 1) * TB,
                                                  kc * 128:(kc + 1) * 128])
                        for kc in range(16)]
            def ld_gb(E, t=t):
                return [E.dma_start_transpose(INB[:, kc * TB:(kc + 1) * TB],
                                              gtm[t * TB:(t + 1) * TB,
                                                  kc * 128:(kc + 1) * 128])
                        for kc in range(16, 32)]
            L.add("Y", ld_ga, waits=((f"d1mm{t-1}_{NE-1}",) if t >= 1 else ()),
                  name=f"d1ga{t}", inc=16 * 16)
            L.add("Y", ld_gb, waits=(), name=f"d1g{t}", inc=16 * 16)
            def ld_xr(E, t=t):
                return E.dma_start(out=XR[:, :].rearrange("p (c t) -> p c t", c=NE),
                                   in_=x1f32[:, t * TB:(t + 1) * TB].rearrange("(c p) t -> p c t", p=128))
            L.add("Y", ld_xr, waits=((f"d1fin{t-1}",) if t >= 1 else ()), name=f"d1xr{t}", inc=16)
            for e in range(NE):
                def ld_w(E, t=t, e=e):
                    return E.dma_start(out=WQS[e % 2][:, :].rearrange("p (n c) -> p n c", n=32),
                                       in_=woq[e].rearrange("(n p) c -> p n c", p=128))
                L.add("Y", ld_w, waits=(f"d0s{NE-1}",) + ((f"d1mm{t}_{e-2}",) if e >= 2 else
                                                          ((f"d1mm{t-1}_{NE-2}", f"d1mm{t-1}_{NE-1}")
                                                           if t >= 1 else ())),
                      name=f"d1w{t}_{e}", inc=16)
                bank, rw = txn()
                def t_mma(E, t=t, e=e, bank=bank):
                    last = None
                    for kc in range(16):
                        last = E.matmul(bank[:, 0:TB], WQS[e % 2][:, kc * 128:(kc + 1) * 128],
                                        INB[:, kc * TB:(kc + 1) * TB],
                                        start=(kc == 0), stop=False)
                    return last
                def t_mmb(E, t=t, e=e, bank=bank):
                    last = None
                    for kc in range(16, 32):
                        last = E.matmul(bank[:, 0:TB], WQS[e % 2][:, kc * 128:(kc + 1) * 128],
                                        INB[:, kc * TB:(kc + 1) * TB],
                                        start=False, stop=(kc == 31))
                    return last
                L.add("T", t_mma, waits=(f"d1w{t}_{e}", f"d1ga{t}", rw), name=f"d1mma{t}_{e}")
                L.add("T", t_mmb, waits=(f"d1g{t}",), name=f"d1mm{t}_{e}")
                def v_res(E, t=t, e=e, bank=bank):
                    return E.tensor_tensor(XR[:, e * TB:(e + 1) * TB], bank[:, 0:TB],
                                           XR[:, e * TB:(e + 1) * TB], op=A.add)
                L.add("V", v_res, waits=(f"d1mm{t}_{e}", f"d1xr{t}"), name=f"d1r{t}_{e}")
                done(f"d1r{t}_{e}")
            fin = emit_ln(f"d1L{t}_", lambda e, t=t: f"d1r{t}_{e}",
                          (f"d1L{t-1}_ssq{NE-1}" if t >= 1 else "memsets"),
                          (f"d1st{t-1}" if t >= 1 else "memsets"), xab=True)
            def st_f(E, t=t):
                return E.dma_start(out=x2f[:, t * TB:(t + 1) * TB].rearrange("(c p) t -> p c t", p=128),
                                   in_=XA[:, :].rearrange("p (c t) -> p c t", c=NE))
            def st_b(E, t=t):
                return E.dma_start(out=x2b[:, t * TB:(t + 1) * TB].rearrange("(c p) t -> p c t", p=128),
                                   in_=XAB[:, :].rearrange("p (c t) -> p c t", c=NE))
            L.add("G", st_f, waits=(fin,), name=f"d1stf{t}", inc=16)
            L.add("G", st_b, waits=(fin,), name=f"d1st{t}", inc=16)
            L.add("V", lambda E: E.tensor_copy(SQ[0][:, 0:1], SQ[0][:, 0:1]),
                  waits=(fin,), name=f"d1fin{t}")

        emit_phase()
        es1.close()

        if stop_after == "d1":
            def fin1(E):
                E.wait_ge(sems["G"], L.evt[f"d1st{NBD-1}"][1])
            block.sync(fin1)
            return nc

        es2 = contextlib.ExitStack()
        es2.__enter__()
        sb2 = lambda name, shape, dt: es2.enter_context(nc.sbuf_tensor(name, shape, dt))
        QW = sb2("QW", [128, 2 * HFF], F32)
        QWB = sb2("QWB", [128, HFF], BF16)

        NWC = {0: 16, 1: 16, 2: NHE}
        WSRC = {0: w1t, 1: w2t, 2: w3t}
        WDST = {0: w1q, 1: w2q, 2: w3q}
        WCOLS = {0: HFF, 1: HFF, 2: D}
        for m in range(3):
            for ch_ in range(NWC[m]):
                def ld_wc(E, m=m, ch_=ch_):
                    return E.dma_start(out=QW[:, (ch_ % 2) * HFF:(ch_ % 2) * HFF + WCOLS[m]],
                                       in_=WSRC[m][ch_ * 128:(ch_ + 1) * 128, :])
                L.add("Y", ld_wc, waits=((f"d2a{m}_{ch_-2}",) if ch_ >= 2 else
                                         ((f"d2a{m-1}_{NWC[m-1]-2}", f"d2a{m-1}_{NWC[m-1]-1}")
                                          if m >= 1 else (f"d1mm{NB-1}_{NE-1}",))),
                      name=f"d2l{m}_{ch_}", inc=16)
                def g_acc(E, m=m, ch_=ch_):
                    chv(E, E.tensor_reduce(GAC[:, 3:4],
                                           QW[:, (ch_ % 2) * HFF:(ch_ % 2) * HFF + WCOLS[m]],
                                           axis=AX, op=A.add, apply_absolute_value=True))
                    return E.tensor_tensor(GAC[:, m:m + 1], GAC[:, m:m + 1], GAC[:, 3:4], op=A.add)
                L.add("V", g_acc, waits=(f"d2l{m}_{ch_}", "memsets",
                                         f"d2a{m}_{ch_-1}" if ch_ >= 1 else
                                         (f"d2a{m-1}_{NWC[m-1]-1}" if m >= 1 else None)),
                      name=f"d2a{m}_{ch_}", inc=2)
        bank, rw = txn()
        def t_gam(E, bank=bank):
            return E.matmul(bank[0:1, 0:4], ONEK[:, :], GAC[:, 0:4], start=True, stop=True)
        L.add("T", t_gam, waits=tuple(f"d2a{m}_{NWC[m]-1}" for m in range(3)) + (rw,),
              name="d2gam")
        def v_gam(E, bank=bank):
            chv(E, E.tensor_copy(GSC[:, :], bank[0:1, 0:4]))
            chv(E, E.tensor_scalar(GSC[0:1, 0:1], GSC[0:1, 0:1], 1.0 / (D * HFF), None, op0=A.mult))
            chv(E, E.tensor_scalar(GSC[0:1, 1:2], GSC[0:1, 1:2], 1.0 / (D * HFF), None, op0=A.mult))
            chv(E, E.tensor_scalar(GSC[0:1, 2:3], GSC[0:1, 2:3], 1.0 / (D * HFF), None, op0=A.mult))
            chv(E, E.tensor_scalar(GSC[0:1, 0:3], GSC[0:1, 0:3], 1e-8, None, op0=A.add))
            return E.reciprocal(GSC[0:1, 0:3], GSC[0:1, 0:3])
        L.add("V", v_gam, waits=("d2gam",), name="d2invg", inc=6)
        done("d2invg")
        bank2, rw2 = txn()
        def t_ibc(E, bank2=bank2):
            return E.matmul(bank2[:, 0:4], ONE1F[:, :], GSC[0:1, 0:4], start=True, stop=True)
        L.add("T", t_ibc, waits=("d2invg", rw2), name="d2ibc")
        L.add("V", lambda E, bank2=bank2: E.tensor_copy(INVG[:, :], bank2[:, 0:4]),
              waits=("d2ibc",), name="d2ibce")
        done("d2ibce")
        for m in range(3):
            for ch_ in range(NWC[m]):
                def ld_wc2(E, m=m, ch_=ch_):
                    return E.dma_start(out=QW[:, (ch_ % 2) * HFF:(ch_ % 2) * HFF + WCOLS[m]],
                                       in_=WSRC[m][ch_ * 128:(ch_ + 1) * 128, :])
                L.add("Y", ld_wc2, waits=("d2ibce",) + ((f"d2q{m}_{ch_-2}",) if ch_ >= 2 else
                                                        ((f"d2q{m-1}_{NWC[m-1]-2}",
                                                          f"d2q{m-1}_{NWC[m-1]-1}") if m >= 1 else ())),
                      name=f"d2l2{m}_{ch_}", inc=16)
                def v_q2(E, m=m, ch_=ch_):
                    o = (ch_ % 2) * HFF
                    n = WCOLS[m]
                    chv(E, E.tensor_scalar(QW[:, o:o + n], QW[:, o:o + n], INVG[:, m:m + 1],
                                           None, op0=A.mult))
                    chv(E, E.tensor_scalar(QWB[:, 0:n], QW[:, o:o + n], 0.5, None, op0=A.is_ge))
                    chv(E, E.tensor_scalar(QW[:, o:o + n], QW[:, o:o + n], -0.5, None, op0=A.is_le))
                    return E.tensor_tensor(QWB[:, 0:n], QWB[:, 0:n], QW[:, o:o + n], op=A.subtract)
                L.add("V", v_q2, waits=(f"d2l2{m}_{ch_}", f"d2st{m}_{ch_-1}" if ch_ >= 1 else
                                        (f"d2st{m-1}_{NWC[m-1]-1}" if m >= 1 else "d2ibce")),
                      name=f"d2q{m}_{ch_}", inc=4)
                def st_q(E, m=m, ch_=ch_):
                    return E.dma_start(out=WDST[m][ch_ * 128:(ch_ + 1) * 128, :],
                                       in_=QWB[:, 0:WCOLS[m]])
                L.add("G", st_q, waits=(f"d2q{m}_{ch_}",), name=f"d2st{m}_{ch_}", inc=16)

        emit_phase()
        es2.close()

        es3 = contextlib.ExitStack()
        es3.__enter__()
        sb3 = lambda name, shape, dt: es3.enter_context(nc.sbuf_tensor(name, shape, dt))
        INB2 = sb3("INB2", [128, NE * TB], BF16)
        WH = [sb3(f"WH{i}", [128, NE * 128], BF16) for i in range(4)]
        W3S = [sb3(f"W3S{i}", [128, NHE * 128], BF16) for i in range(2)]
        HST = sb3("HST", [128, NHE * TB], BF16)

        for t in range(NB):
            def ld_x2(E, t=t):
                return E.dma_start(out=INB2[:, :].rearrange("p (c t) -> p c t", c=NE),
                                   in_=x2b[:, t * TB:(t + 1) * TB].rearrange("(c p) t -> p c t", p=128))
            L.add("Y", ld_x2, waits=(f"d1st{t}",) +
                  ((f"d3h1m{t-1}_{NHE-1}", f"d3h2m{t-1}_{NHE-1}") if t >= 1 else
                   (f"d2st2_{NHE-1}",)),
                  name=f"d3x{t}", inc=16)
            def ld_xr2(E, t=t):
                return E.dma_start(out=XR[:, :].rearrange("p (c t) -> p c t", c=NE),
                                   in_=x2f[:, t * TB:(t + 1) * TB].rearrange("(c p) t -> p c t", p=128))
            L.add("Y", ld_xr2, waits=(f"d1stf{t}",) + ((f"d3fin{t-1}",) if t >= 1 else ()),
                  name=f"d3xr{t}", inc=16)
            for he in range(NHE):
                hb = he % 2
                def ld_w1(E, t=t, he=he, hb=hb):
                    return E.dma_start(out=WH[hb][:, :].rearrange("p (k c) -> p k c", k=NE),
                                       in_=w1q[:, he * 128:(he + 1) * 128]
                                       .rearrange("(k p) c -> p k c", p=128))
                def ld_w2(E, t=t, he=he, hb=hb):
                    return E.dma_start(out=WH[2 + hb][:, :].rearrange("p (k c) -> p k c", k=NE),
                                       in_=w2q[:, he * 128:(he + 1) * 128]
                                       .rearrange("(k p) c -> p k c", p=128))
                gi = t * NHE + he - 2
                wgate = ((f"d3h1m{gi // NHE}_{gi % NHE}",) if gi >= 0 else ())
                wgate2 = ((f"d3h2m{gi // NHE}_{gi % NHE}",) if gi >= 0 else ())
                L.add("Y", ld_w1, waits=wgate + ((f"d2st2_{NHE-1}",) if t == 0 and he == 0 else ()),
                      name=f"d3w1{t}_{he}", inc=16)
                L.add("Y", ld_w2, waits=wgate2 + ((f"d2st2_{NHE-1}",) if t == 0 and he == 0 else ()),
                      name=f"d3w2{t}_{he}", inc=16)
                b1, rw1 = txn()
                def t_h1(E, t=t, he=he, b1=b1, hb=hb):
                    last = None
                    for kc in range(NE):
                        last = E.matmul(b1[:, 0:TB], WH[hb][:, kc * 128:(kc + 1) * 128],
                                        INB2[:, kc * TB:(kc + 1) * TB],
                                        start=(kc == 0), stop=(kc == NE - 1))
                    return last
                L.add("T", t_h1, waits=(f"d3w1{t}_{he}", f"d3x{t}", rw1), name=f"d3h1m{t}_{he}")
                b2, rw2 = txn()
                def t_h2(E, t=t, he=he, b2=b2, hb=hb):
                    last = None
                    for kc in range(NE):
                        last = E.matmul(b2[:, 0:TB], WH[2 + hb][:, kc * 128:(kc + 1) * 128],
                                        INB2[:, kc * TB:(kc + 1) * TB],
                                        start=(kc == 0), stop=(kc == NE - 1))
                    return last
                L.add("T", t_h2, waits=(f"d3w2{t}_{he}", f"d3x{t}", rw2), name=f"d3h2m{t}_{he}")
                def c_sig(E, b1=b1):
                    return E.activation(SQ[0][:, :], b1[:, 0:TB], AF.Sigmoid)
                L.add("C", c_sig, waits=(f"d3h1m{t}_{he}",
                                         f"d3hid{t}_{he-1}" if he >= 1 else
                                         (f"d3hid{t-1}_{NHE-1}" if t >= 1 else "memsets")),
                      name=f"d3sig{t}_{he}")
                done(f"d3sig{t}_{he}")
                def v_hid(E, t=t, he=he, b2=b2):
                    return E.tensor_tensor(HST[:, he * TB:(he + 1) * TB], SQ[0][:, :],
                                           b2[:, 0:TB], op=A.mult)
                L.add("V", v_hid, waits=(f"d3sig{t}_{he}", f"d3h2m{t}_{he}",
                                         f"d3om{t-1}_{NE-1}" if (he == 0 and t >= 1) else None),
                      name=f"d3hid{t}_{he}")
                done(f"d3hid{t}_{he}")
            for e in range(NE):
                def ld_w3(E, t=t, e=e):
                    return E.dma_start(out=W3S[e % 2][:, :].rearrange("p (k c) -> p k c", k=NHE),
                                       in_=w3q[:, e * 128:(e + 1) * 128]
                                       .rearrange("(k p) c -> p k c", p=128))
                L.add("Y", ld_w3, waits=((f"d3om{t}_{e-2}",) if e >= 2 else
                                         ((f"d3om{t-1}_{NE-2}", f"d3om{t-1}_{NE-1}") if t >= 1
                                          else (f"d2st2_{NHE-1}",))),
                      name=f"d3w3{t}_{e}", inc=16)
                bank, rw = txn()
                def t_o(E, t=t, e=e, bank=bank):
                    last = None
                    for kc in range(NHE):
                        last = E.matmul(bank[:, 0:TB], W3S[e % 2][:, kc * 128:(kc + 1) * 128],
                                        HST[:, kc * TB:(kc + 1) * TB],
                                        start=(kc == 0), stop=(kc == NHE - 1))
                    return last
                L.add("T", t_o, waits=(f"d3w3{t}_{e}", rw) +
                      tuple(f"d3hid{t}_{kc}" for kc in range(NHE)), name=f"d3om{t}_{e}")
                def v_res2(E, t=t, e=e, bank=bank):
                    return E.tensor_tensor(XR[:, e * TB:(e + 1) * TB], bank[:, 0:TB],
                                           XR[:, e * TB:(e + 1) * TB], op=A.add)
                L.add("V", v_res2, waits=(f"d3om{t}_{e}", f"d3xr{t}"), name=f"d3r{t}_{e}")
                done(f"d3r{t}_{e}")
            fin2 = emit_ln(f"d3L{t}_", lambda e, t=t: f"d3r{t}_{e}",
                           (f"d3L{t-1}_ssq{NE-1}" if t >= 1 else f"d1L{NB-1}_ssq{NE-1}"),
                           (f"d3st{t-1}" if t >= 1 else f"d1st{NB-1}"), xab=False)
            def st_x3(E, t=t):
                return E.dma_start(out=x3o[:, t * TB:(t + 1) * TB].rearrange("(c p) t -> p c t", p=128),
                                   in_=XA[:, :].rearrange("p (c t) -> p c t", c=NE))
            L.add("G", st_x3, waits=(fin2,), name=f"d3st{t}", inc=16)
            L.add("V", lambda E: E.tensor_copy(SQ[0][:, 0:1], SQ[0][:, 0:1]),
                  waits=(fin2,), name=f"d3fin{t}")

        emit_phase()
        es3.close()

        def fin_sync(E):
            E.wait_ge(sems["G"], L.evt[f"d3st{NB-1}"][1])
        block.sync(fin_sync)
    return nc


def prep_l1f_inputs(inputs, x1_outs=None, gated_res=None, n_cores=8):
    Wout = inputs["L1_attn_W_out"]
    wout_et = np.ascontiguousarray(
        np.ascontiguousarray(Wout.T).reshape(4096, NE, 128).transpose(1, 0, 2))
    w1t = np.ascontiguousarray(inputs["L1_ffn_w1"].T)
    w2t = np.ascontiguousarray(inputs["L1_ffn_w2"].T)
    w3t = np.ascontiguousarray(inputs["L1_ffn_w3"].T)
    maps = []
    nco = len(x1_outs) if x1_outs is not None else n_cores
    for core in range(nco):
        b, j = divmod(core, 2)
        m = dict(wout_et=wout_et, w1t=w1t, w2t=w2t, w3t=w3t)
        if x1_outs is not None:
            gl = np.asarray(gated_res[2 * b])
            gr = np.asarray(gated_res[2 * b + 1])
            gfull = np.concatenate([gl, gr], axis=1)
            gslice = gfull[j * NT:(j + 1) * NT, :]
            m["gtm"] = np.ascontiguousarray(gslice)
            m["x1f32"] = np.ascontiguousarray(x1_outs[core])
        maps.append(m)
    return maps


def run_l1f(inputs, x1_outs, gated_res, n_cores=8):
    nc = bass.Bass("TRN2", target_bir_lowering=False, detect_race_conditions=False)
    build_l1f(nc)
    maps = prep_l1f_inputs(inputs, x1_outs, gated_res)[:n_cores]
    res = run_bass_kernel_spmd(nc, maps, core_ids=list(range(n_cores)))
    return res.results


class FX:
    def __init__(self, es, block, prefix=""):
        self.es = es
        self.block = block
        self.prefix = prefix
        self.aps = {}


def build_mega(nc, sim=False, upto=None):
    import contextlib
    shkw = {} if sim else {"addr_space": "Shared"}
    xsh = nc.dram_tensor("xsh", [D, S], F32, kind="Internal", **shkw).ap()
    gsh = nc.dram_tensor("gsh", [S, 4096], BF16, kind="Internal", **shkw).ap()
    flsh = nc.dram_tensor("flsh", [2, 8], I32, kind="Internal", **shkw).ap()
    nnc = nc.dram_tensor("nnc", [1, 8], I32, kind="ExternalInput").ap()
    if sim:
        xpart = nc.dram_tensor("xpart", [D, NT], F32, kind="ExternalInput").ap()
        gpart = nc.dram_tensor("gpart", [S, 2048], BF16, kind="ExternalInput").ap()

    es = contextlib.ExitStack()
    with es:
        block = es.enter_context(nc.Block())
        gph = es.enter_context(nc.semaphore("gph"))
        sXC = es.enter_context(nc.semaphore("sXC"))
        fx = FX(es=es, block=block)

        def headwaits(v):
            for emit in (block.sync, block.gpsimd, block.tensor, block.vector,
                         block.scalar):
                emit(lambda E, v=v: E.wait_ge(gph, v))

        def phase_done():
            block.sync(lambda E: E.sem_inc(gph, 1))

        xcnt = [0]

        def exchange(E, gate, pair_out, src, simdst, k, post=None):
            E.wait_ge(gph, gate)
            for arm in E.switch_core_id(n=128):
                j = arm.logical % 2
                E.dma_start(out=pair_out(j), in_=src[:, :]).then_inc(sXC, 16)
            xcnt[0] += 16
            if sim:
                E.dma_start(out=simdst[0], in_=simdst[1][:, :]).then_inc(sXC, 16)
                xcnt[0] += 16
            E.wait_ge(sXC, xcnt[0])
            with E.register(f"rn{k}") as rn, E.register(f"rf{k}") as rf, \
                    E.register(f"rd{k}") as rd:
                E.load(rn, nnc[0:1, k:k + 1])
                for arm in E.switch_core_id(n=128):
                    j = arm.logical % 2
                    E.store(flsh[j:j + 1, k:k + 1], rn)
                for slot in range(1 if sim else 2):
                    E.load(rf, flsh[slot:slot + 1, k:k + 1])
                    E.reg_sub(rd, rf, rn)
                    with E.While(rd):
                        E.load(rf, flsh[slot:slot + 1, k:k + 1])
                        E.reg_sub(rd, rf, rn)
            if post is not None:
                for arm in E.switch_core_id(n=128):
                    j = arm.logical % 2
                    o, i = post(j)
                    E.dma_start(out=o, in_=i).then_inc(sXC, 16)
                xcnt[0] += 16
                E.wait_ge(sXC, xcnt[0])
            E.sem_inc(gph, 1)

        fx.prefix = "A"
        build_l0(nc, fx=fx)
        phase_done()
        x1ap = fx.aps["x1"]

        def dump(ap, gate, name, shape, dt_):
            dbg = nc.dram_tensor(name, shape, dt_, kind="ExternalOutput").ap()
            sdb = es.enter_context(nc.semaphore("sdb" + name))
            def body(E):
                E.wait_ge(gph, gate)
                E.dma_start(out=dbg, in_=ap).then_inc(sdb, 16)
                E.wait_ge(sdb, 16)
            block.sync(body)

        if upto == "l0":
            dump(x1ap, 1, "dbg_x1", [D, NT], F32)
            return nc

        def x0_body(E):
            exchange(E, 1, lambda j: xsh[:, j * NT:(j + 1) * NT], x1ap,
                     (xsh[:, NT:2 * NT], xpart) if sim else None, 0)
        block.gpsimd(x0_body)

        if upto == "x0":
            dump(xsh, 2, "dbg_xsh", [D, S], F32)
            return nc

        headwaits(2)
        fx.prefix = "B"
        fx.aps["x1f"] = xsh
        build_l1p(nc, fx=fx)
        phase_done()

        if upto == "l1p":
            dump(fx.aps["qc"], 3, "dbg_qc", [NQE * 128, S], BF16)
            return nc

        headwaits(3)
        fx.prefix = "C"
        build_l1s(nc, fx=fx)
        phase_done()

        if upto == "l1s":
            dump(fx.aps["gated"], 4, "dbg_gated", [S, D], BF16)
            return nc

        gap = fx.aps["gated"]
        gloc2 = nc.dram_tensor("gloc2", [NT, 4096], BF16, kind="Internal").ap()

        def x1_body(E):
            exchange(E, 4, lambda j: gsh[:, 2048 * j:2048 * (j + 1)], gap,
                     (gsh[:, 2048:4096], gpart) if sim else None, 1,
                     post=lambda j: (gloc2, gsh[j * NT:(j + 1) * NT, :]))
        block.gpsimd(x1_body)

        if upto == "x1":
            dump(gsh, 5, "dbg_gsh", [S, 4096], BF16)
            return nc

        headwaits(5)
        fx.prefix = "D"
        fx.aps["gtm"] = gloc2
        fx.aps["x1f32"] = x1ap
        build_l1f(nc, fx=fx)
    return nc


def prep_l0_inputs(inputs):
    x = inputs["L0_attn_x"]
    def etile_major(w):
        return np.ascontiguousarray(np.ascontiguousarray(w.T).reshape(D, NE, 128).transpose(1, 0, 2))
    def ktile_major(w):
        return np.ascontiguousarray(w.T).reshape(NE, 128, D)
    wq = etile_major(inputs["L0_attn_Wq"]); wk = etile_major(inputs["L0_attn_Wk"])
    wv = etile_major(inputs["L0_attn_Wv"]); wo = ktile_major(inputs["L0_attn_Wo"])
    wf = ktile_major(inputs["L0_ffn_weight"])
    fscale = np.ascontiguousarray(inputs["L0_ffn_scale_weight"].reshape(NE, 128).T)
    fbias = np.ascontiguousarray(inputs["L0_ffn_bias"].reshape(NE, 128).T)
    ident = np.eye(128).astype(BF)
    t1 = np.arange(128) // 16
    maskadd = np.where(t1[:, None] == t1[None, :], 0.0, -30000.0).astype(np.float32)
    maps = []
    for core in range(8):
        b, j = divmod(core, 2)
        x0t = np.ascontiguousarray(x[b, j * NT:(j + 1) * NT, :].T)
        maps.append(dict(x0=x0t, wq=wq, wk=wk, wv=wv, wo=wo, wf=wf,
                         fscale=fscale, fbias=fbias, ident=ident, maskadd=maskadd))
    return maps


def run_l0(inputs, n_cores=8, debug=False, stop_after=None):
    nc = bass.Bass("TRN2", target_bir_lowering=False, detect_race_conditions=False)
    build_l0(nc, debug=debug, stop_after=stop_after)
    maps = prep_l0_inputs(inputs)[:n_cores]
    res = run_bass_kernel_spmd(nc, maps, core_ids=list(range(n_cores)))
    if debug:
        return res.results
    return [np.asarray(r["x1"], np.float32) for r in res.results]


def _core_feat_idx(j):
    return np.r_[1024 * j:1024 * j + 1024, 2048 + 1024 * j:2048 + 1024 * j + 1024,
                 4096 + 2048 * j:4096 + 2048 * j + 2048]


def prep_l1p_inputs(inputs, x1_outs=None):
    Wqkv = inputs["L1_attn_W_qkv"]; Wz = inputs["L1_attn_W_z"]
    Wb = inputs["L1_attn_W_b"]; Wa = inputs["L1_attn_W_a"]
    cwf = inputs["L1_attn_conv_w"][:, 0, :]
    maps = []
    for core in range(8):
        b, j = divmod(core, 2)
        fidx = _core_feat_idx(j)
        wq_s = Wqkv[fidx]
        wqkv_p = np.ascontiguousarray(
            np.ascontiguousarray(wq_s.T).reshape(D, NQE, 128).transpose(1, 0, 2))
        wz_s = Wz[2048 * j:2048 * j + 2048]
        wz_p = np.ascontiguousarray(
            np.ascontiguousarray(wz_s.T).reshape(D, NZE, 128).transpose(1, 0, 2))
        wba_s = np.concatenate([Wb[16 * j:16 * j + 16], Wa[16 * j:16 * j + 16]], 0)
        wba_p = np.ascontiguousarray(wba_s.T).reshape(NE, 128, 32)
        cw_s = cwf[fidx]
        cw_p = np.ascontiguousarray(cw_s.reshape(NQE, 128, 4).transpose(1, 0, 2).reshape(128, 128))
        m = dict(wqkv=wqkv_p, wz=wz_p, wba=wba_p, cw=cw_p)
        if x1_outs is not None:
            x1f = np.concatenate([x1_outs[2 * b], x1_outs[2 * b + 1]], axis=1)
            m["x1f"] = np.ascontiguousarray(x1f)
        maps.append(m)
    return maps


def run_l1p(inputs, x1_outs, n_cores=8):
    nc = bass.Bass("TRN2", target_bir_lowering=False, detect_race_conditions=False)
    build_l1p(nc)
    maps = prep_l1p_inputs(inputs, x1_outs)[:n_cores]
    res = run_bass_kernel_spmd(nc, maps, core_ids=list(range(n_cores)))
    return res.results


def prep_l1s_inputs(inputs, l1p_res=None, n_cores=8):
    ncores_avail = len(l1p_res) if l1p_res is not None else n_cores
    A_log = inputs["L1_attn_A_log"]; dtb = inputs["L1_attn_dt_bias"]
    nwv = inputs["L1_attn_norm_weight"]
    maps = []
    for core in range(ncores_avail):
        b, j = divmod(core, 2)
        adt = np.concatenate([np.tile(dtb[16 * j:16 * j + 16], (128, 1)),
                              np.tile(A_log[16 * j:16 * j + 16], (128, 1))], 1).astype(np.float32)
        nw = np.tile(nwv, (128, 1)).astype(np.float32)
        one = np.ones((128, 128), np.float32)
        m0 = {}
        if l1p_res is not None:
            r = l1p_res[core]
            m0 = dict(qc=np.asarray(r["qc"]), zsf=np.asarray(r["zsf"]),
                      ba=np.asarray(r["ba"]))
        maps.append(dict(
            **m0,
            adt=adt, nw=nw,
            msl=np.ascontiguousarray(np.tril(one, -1)),
            msu=np.ascontiguousarray(np.triu(one, 1)),
            muo=np.ascontiguousarray(np.triu(one, 0)),
            identb=np.eye(128).astype(BF),
            identf=np.eye(128).astype(np.float32)))
    return maps


def run_l1s(inputs, l1p_res, n_cores=8, nch=NCH, debug=False):
    nc = bass.Bass("TRN2", target_bir_lowering=False, detect_race_conditions=False)
    build_l1s(nc, nch=nch, debug=debug)
    maps = prep_l1s_inputs(inputs, l1p_res)[:n_cores]
    res = run_bass_kernel_spmd(nc, maps, core_ids=list(range(n_cores)))
    return res.results


def prep_mega_inputs(inputs, n_cores=8):
    m0 = prep_l0_inputs(inputs)
    m1 = prep_l1p_inputs(inputs)
    m2 = prep_l1s_inputs(inputs, n_cores=n_cores)
    m3 = prep_l1f_inputs(inputs, n_cores=n_cores)
    rng = np.random.default_rng()
    nonce = rng.integers(1, 2 ** 30, size=(1, 8), dtype=np.int32)
    maps = []
    for core in range(n_cores):
        m = {}
        m.update(m0[core])
        m.update(m1[core])
        m.update(m2[core])
        m.update(m3[core])
        m["nnc"] = nonce
        maps.append(m)
    return maps


def kernel(**inputs):
    nc = bass.Bass("TRN2", target_bir_lowering=False, detect_race_conditions=False)
    build_mega(nc)
    maps = prep_mega_inputs(inputs)
    res = run_bass_kernel_spmd(nc, maps, core_ids=list(range(8)))
    full = np.zeros((4, S, D), np.float32)
    for core in range(8):
        b, j = divmod(core, 2)
        full[b, j * NT:(j + 1) * NT, :] = np.asarray(res.results[core]["x3"],
                                                     np.float32).T
    return full


def kernel_unfused(**inputs):
    x1_outs = run_l0(inputs)
    l1p_res = run_l1p(inputs, x1_outs)
    l1s_res = run_l1s(inputs, l1p_res)
    gated_res = [np.asarray(r["gated"]) for r in l1s_res]
    f_res = run_l1f(inputs, x1_outs, gated_res)
    full = np.zeros((4, S, D), np.float32)
    for core in range(8):
        b, j = divmod(core, 2)
        full[b, j * NT:(j + 1) * NT, :] = np.asarray(f_res[core]["x3"], np.float32).T
    return full
